# version 08 (final)
# speedup vs baseline: 1.1009x; 1.0134x over previous
import contextlib

import numpy as np
import ml_dtypes

import concourse.bass as bass
import concourse.mybir as mybir
from concourse.bass_utils import run_bass_kernel_spmd

F32 = mybir.dt.float32
BF16 = mybir.dt.bfloat16
AF = mybir.ActivationFunctionType
ALU = mybir.AluOpType
AX = mybir.AxisListType
BF = ml_dtypes.bfloat16

D = 1024
NCORES = 8


_CONST_CACHE = {}


def host_consts():
    if _CONST_CACHE:
        return _CONST_CACHE
    k = np.arange(513)
    e = np.arange(D)
    ang = 2.0 * np.pi * np.outer(e, k) / D
    ff = np.zeros((D, 1280), np.float64)
    ff[:, 0:513] = np.cos(ang)
    ff[:, 640:1153] = -np.sin(ang)
    _CONST_CACHE["c_ff"] = ff.astype(BF)

    wk = np.full(513, 2.0)
    wk[0] = 1.0
    wk[512] = 1.0
    dd = np.arange(D)
    ang2 = 2.0 * np.pi * np.outer(k, dd) / D
    idft = np.zeros((1280, D), np.float64)
    idft[0:513, :] = (wk[:, None] / D) * np.cos(ang2)
    idft[640:1153, :] = -(wk[:, None] / D) * np.sin(ang2)
    _CONST_CACHE["c_idft"] = idft.astype(BF)

    _CONST_CACHE["c_ident"] = np.eye(128, dtype=BF)
    _CONST_CACHE["c_identf"] = np.eye(128, dtype=np.float32)

    s = np.arange(128)[:, None]
    t = np.arange(128)[None, :]
    _CONST_CACHE["c_mtril16"] = (((s // 16) == (t // 16)) & (t >= s)
                                 ).astype(np.float32)
    _CONST_CACHE["c_m16off"] = (((s // 16) == (t // 16) - 1)
                                & (((t // 16) % 2) == 1)).astype(np.float32)
    _CONST_CACHE["c_mpair"] = (((s // 64) == (t // 64))).astype(np.float32)
    mc = np.zeros((128, 2), np.float32)
    mc[0:64, 0] = 1.0
    mc[64:128, 1] = 1.0
    _CONST_CACHE["c_mcols"] = mc
    return _CONST_CACHE


class Prog:
    ENGINES = ("t", "v", "s", "g", "q")
    _ATTR = {"t": "tensor", "v": "vector", "s": "scalar", "g": "gpsimd",
             "q": "sync"}

    def __init__(self, nc, es):
        self.nc = nc
        self.es = es
        self.sem = {}
        self.cnt = {}
        self.wm = {e: {} for e in self.ENGINES}
        for e in self.ENGINES:
            self._mksem("ep_" + e)
        self._dma_id = 0

    def _mksem(self, name):
        self.sem[name] = self.es.enter_context(self.nc.semaphore(name))
        self.cnt[name] = 0
        return name

    def dma_sem(self, tag=""):
        self._dma_id += 1
        return self._mksem(f"dma{self._dma_id}_{tag}")

    def eng(self, e):
        return getattr(self.nc, self._ATTR[e])

    def emit(self, e, fn, w=(), inc=False):
        agg = {}
        for tokn in w:
            if tokn is None:
                continue
            s, v = tokn
            if self.wm[e].get(s, 0) < v:
                agg[s] = max(agg.get(s, 0), v)
        need = sorted(agg.items())
        eng = self.eng(e)
        for s, v in need[:-1]:
            eng.wait_ge(self.sem[s], v)
            self.wm[e][s] = v
        inst = fn(eng)
        if need:
            s, v = need[-1]
            inst._wait_ge(self.sem[s], v)
            self.wm[e][s] = v
        if inc:
            ep = "ep_" + e
            inst.then_inc(self.sem[ep], 1)
            self.cnt[ep] += 1
            return (ep, self.cnt[ep])
        return None

    def wait(self, e, w):
        for tokn in w:
            if tokn is None:
                continue
            s, v = tokn
            if self.wm[e].get(s, 0) < v:
                self.eng(e).wait_ge(self.sem[s], v)
                self.wm[e][s] = v

    def tok(self, e):
        return ("ep_" + e, self.cnt["ep_" + e])

    def dma(self, e, out, in_, sem_name, w=(), **kw):
        agg = {}
        for tokn in w:
            if tokn is None:
                continue
            s, v = tokn
            if self.wm[e].get(s, 0) < v:
                agg[s] = max(agg.get(s, 0), v)
        need = sorted(agg.items())
        eng = self.eng(e)
        for s, v in need[:-1]:
            eng.wait_ge(self.sem[s], v)
            self.wm[e][s] = v
        inst = eng.dma_start(out, in_, **kw)
        if need:
            s, v = need[-1]
            inst._wait_ge(self.sem[s], v)
            self.wm[e][s] = v
        inst.then_inc(self.sem[sem_name], 16)
        self.cnt[sem_name] += 16
        return (sem_name, self.cnt[sem_name])

    def barrier(self):
        toks = [(s, c) for s, c in self.cnt.items()
                if c > 0 and s != "ep_q"]
        self.wait("q", toks)
        self.eng("q").nop().then_inc(self.sem["ep_q"], 1)
        self.cnt["ep_q"] += 1
        qt = [("ep_q", self.cnt["ep_q"])]
        for e in ("t", "v", "s", "g"):
            self.wait(e, qt)
            for s, c in toks:
                self.wm[e][s] = max(self.wm[e].get(s, 0), c)
        for s, c in toks:
            self.wm["q"][s] = max(self.wm["q"].get(s, 0), c)


class PsumPool:

    def __init__(self, es, nc, name, n, shape, dt=F32):
        self.tiles = [
            es.enter_context(nc.psum_tensor(f"{name}{i}", shape, dt))
            for i in range(n)
        ]
        self.free_tok = [None] * n
        self.i = 0

    def get(self):
        i = self.i
        self.i = (self.i + 1) % len(self.tiles)
        return i, self.tiles[i], self.free_tok[i]

    def done(self, i, tok):
        self.free_tok[i] = tok


def mm_transpose(p, psum_ap, src_ap, ident_ap, w=(), inc=True):
    return p.emit(
        "t",
        lambda t: t.matmul(psum_ap, src_ap, ident_ap, start=True, stop=True),
        w=w,
        inc=inc,
    )


def emit_ln_tile(p, xsrc_ap, lnt_ap, stat, w_src, w_lnt=(), epscol=None):
    t1 = p.emit("v", lambda v: v.bn_stats(stat[:, 0:6], xsrc_ap[:, 0:512]),
                w=list(w_src), inc=True)
    t2 = p.emit("v", lambda v: v.bn_stats(stat[:, 6:12], xsrc_ap[:, 512:1024]),
                w=[t1], inc=True)
    t3 = p.emit("v", lambda v: v.bn_aggr(stat[:, 12:14], stat[:, 0:12]),
                w=[t2], inc=True)
    t4 = p.emit("s", lambda s: s.activation(stat[:, 14:15], stat[:, 13:14],
                                            AF.Ln, bias=epscol),
                w=[t3], inc=True)
    t5 = p.emit("s", lambda s: s.activation(stat[:, 14:15], stat[:, 14:15],
                                            AF.Exp, scale=-0.5),
                w=[t4], inc=True)
    t7 = p.emit("v", lambda v: v.tensor_scalar(
        lnt_ap, xsrc_ap, stat[:, 12:13], stat[:, 14:15],
        op0=ALU.subtract, op1=ALU.mult),
        w=[t5] + list(w_lnt), inc=True)
    return t7


def emit_l0_attn(p, es, T, xin, xout, W, C):
    nc = p.nc
    TC = 256
    NCH = T // TC
    NF = 10

    sb = lambda name, shape, dt: es.enter_context(
        nc.sbuf_tensor(name, shape, dt))

    ident = sb("a_ident", [128, 128], BF16)
    identf = sb("a_identf", [128, 128], F32)
    M = [sb(f"a_M{s}", [128, 8, 1280], BF16) for s in range(3)]
    bfcol = sb("a_bfcol", [128, 3, 10], F32)
    xt = [sb(f"a_xt{i}", [128, 2, 1024], F32) for i in range(3)]
    lnt = sb("a_lnt", [128, 1024], F32)
    xlnT = sb("a_xlnT", [128, 8, TC], BF16)
    qf2 = [sb("a_qf0", [128, NF, TC], F32)]
    kf2 = [sb("a_kf0", [128, NF, TC], F32)]
    vf2 = [sb("a_vf0", [128, NF, TC], F32)]
    t4 = [sb(f"a_t4{i}", [128, 5, TC], F32) for i in range(3)]
    _outf = sb("a_outf0", [128, NF, TC], BF16)
    outf2 = [_outf, _outf]
    attdm = sb("a_attdm", [128, 8, TC], F32)
    carry = sb("a_carry", [128, NF], F32)
    zrow = sb("a_zrow", [128, TC], F32)
    stat = sb("a_stat", [128, 16], F32)
    epsln = sb("a_epsln", [128, 1], F32)
    epsfft = sb("a_epsfft", [128, 1], F32)
    eps30 = sb("a_eps30", [128, 1], F32)
    x1t = sb("a_x1t", [128, 1024], F32)

    pool = PsumPool(es, nc, "a_pp", 4, [128, 512], F32)
    poolT = PsumPool(es, nc, "a_pt", 2, [128, 128], F32)
    poolB = PsumPool(es, nc, "a_pb", 1, [128, 8], F32)

    sC = p.dma_sem("ac")
    sW = p.dma_sem("aw")
    sX = [p.dma_sem("ax0"), p.dma_sem("ax1"), p.dma_sem("ax2")]
    sO = p.dma_sem("ao")

    es2 = contextlib.ExitStack()
    c_ff = es2.enter_context(nc.sbuf_tensor("a_cff", [128, 8, 1280], BF16))
    wsig = es2.enter_context(nc.sbuf_tensor("a_wsig", [128, 8, 1024], BF16))
    b8 = es2.enter_context(nc.sbuf_tensor("a_b8", [8, 128], BF16))
    bcol = es2.enter_context(nc.sbuf_tensor("a_bcol", [128, 8], BF16))
    p.dma("q", c_ff[:, :, :],
          C["c_ff"].rearrange("(k p) f -> p k f", p=128), sC)
    p.dma("q", ident[:, :], C["c_ident"], sC)
    tCall = p.dma("q", identf[:, :], C["c_identf"], sC)
    tC1 = tC3 = tC4 = tCall
    p.emit("g", lambda g: g.memset(zrow[:, :], 0.0), inc=True)
    p.emit("g", lambda g: g.memset(epsln[:, :], 1e-5), inc=True)
    p.emit("g", lambda g: g.memset(epsfft[:, :], 1e-8), inc=True)
    tEps = p.emit("g", lambda g: g.memset(eps30[:, :], 1e-30), inc=True)
    tZ = p.emit("g", lambda g: g.memset(carry[:, :], 0.0), inc=True)

    free_wsig = None
    free_bcol = None
    for si, (wn, bn) in enumerate([("Wq", "bq"), ("Wk", "bk"), ("Wv", "bv")]):
        p.dma("g", wsig[:, :, :],
              W[wn].rearrange("(k p) d -> p k d", p=128), sW,
              w=[free_wsig])
        tBd = p.dma("g", b8[:, :], W[bn].rearrange("(a p) -> a p", a=8), sW,
                    w=[free_bcol])
        tWd = tBd
        bi, bps, bwar = poolB.get()
        tb = p.emit("t", lambda t: t.matmul(
            bps[:, 0:8], b8[:, :], ident[0:8, 0:8], start=True, stop=True),
            w=[tBd, tC3, bwar], inc=True)
        tb2 = p.emit("s", lambda s: s.copy(bcol[:, :], bps[:, 0:8]),
                     w=[tb], inc=True)
        poolB.done(bi, tb2)
        tMev = []
        for dkt in range(8):
            for (n0, n1) in ((0, 512), (512, 1024), (1024, 1280)):
                pi, pps, pwar = pool.get()
                tm = None
                for ekt in range(8):
                    tm = p.emit(
                        "t",
                        lambda t, e_=ekt: t.matmul(
                            pps[:, 0:n1 - n0],
                            wsig[:, e_, 128 * dkt:128 * (dkt + 1)],
                            c_ff[:, e_, n0:n1],
                            start=(e_ == 0), stop=(e_ == 7)),
                        w=[tWd, tC1, pwar] if ekt == 0 else (),
                        inc=(ekt == 7))
                te = p.emit("s", lambda s: s.copy(
                    M[si][:, dkt, n0:n1], pps[:, 0:n1 - n0]),
                    w=[tm], inc=True)
                pool.done(pi, te)
                tMev.append(tm)
        free_wsig = tMev[-1]
        tbias = []
        for m in range(NF):
            bi, bps, bwar = poolB.get()
            tm = None
            for ekt in range(8):
                tm = p.emit(
                    "t",
                    lambda t, e_=ekt: t.matmul(
                        bps[:, 0:1],
                        c_ff[:, e_, 128 * m:128 * (m + 1)],
                        bcol[:, e_:e_ + 1],
                        start=(e_ == 0), stop=(e_ == 7)),
                    w=[tb2, bwar] if ekt == 0 else (),
                    inc=(ekt == 7))
            te = p.emit("s", lambda s: s.copy(
                bfcol[:, si, m:m + 1], bps[:, 0:1]), w=[tm], inc=True)
            poolB.done(bi, te)
            tbias.append(tm)
        free_bcol = tbias[-1]

    es2.close()
    c_idft = sb("a_cidft", [128, 10, 1024], BF16)
    sC2 = p.dma_sem("ac2")
    tC2 = p.dma("q", c_idft[:, :, :],
                C["c_idft"].rearrange("(k p) f -> p k f", p=128), sC2,
                w=[p.tok("t"), p.tok("s"), p.tok("v")])
    qf2.append(sb("a_qf1", [128, NF, TC], F32))
    kf2.append(sb("a_kf1", [128, NF, TC], F32))
    vf2.append(sb("a_vf1", [128, NF, TC], F32))

    xin3 = xin.rearrange("(c tt p) d -> c tt p d", tt=2, p=128)
    xout3 = xout.rearrange("(c tt p) d -> c tt p d", tt=2, p=128)

    tX = [None, None, None]
    xfree = [None, None, None]
    tX[0] = p.dma("q", xt[0][:, 0, :], xin3[0, 0], sX[0])
    tX[0] = p.dma("q", xt[0][:, 1, :], xin3[0, 1], sX[0])

    free_xlnT = None
    free_outf = None
    free_attdm = None
    free_lnt = None
    free_x1t = None
    carry_tok = tZ
    tsO = None

    sig_toks = [[None, None, None], [None, None, None]]

    def emit_apply(c):
        nonlocal free_lnt
        cb = c % 2
        xb = xt[c % 3]
        if c + 1 < NCH:
            nb = (c + 1) % 3
            tX[nb] = p.dma("q", xt[nb][:, 0, :], xin3[c + 1, 0], sX[nb],
                           w=[xfree[nb]])
            tX[nb] = p.dma("q", xt[nb][:, 1, :], xin3[c + 1, 1], sX[nb])
        qf, kf, vf = qf2[cb], kf2[cb], vf2[cb]
        txl = []
        for tt in range(2):
            tln = emit_ln_tile(p, xb[:, tt, :], lnt[:, :], stat,
                               w_src=[tX[c % 3], tZ], w_lnt=[free_lnt],
                               epscol=epsln[:, :])
            for m in range(8):
                ti_, tps, twar = poolT.get()
                tp = mm_transpose(p, tps[:, 0:128],
                                  lnt[:, 128 * m:128 * (m + 1)],
                                  identf[:, :],
                                  w=[tln, tC4, twar])
                te = p.emit("s", lambda s, m=m, tt=tt: s.copy(
                    xlnT[:, m, 128 * tt:128 * (tt + 1)], tps[:, 0:128]),
                    w=[tp], inc=True)
                poolT.done(ti_, te)
                txl.append(te)
            free_lnt = tp
        txln = txl[-1]
        sigbuf = [qf, kf, vf]
        for si in range(3):
            for m in range(NF):
                pi, pps, pwar = pool.get()
                tm = None
                for kt in range(8):
                    tm = p.emit(
                        "t",
                        lambda t, k_=kt: t.matmul(
                            pps[:, 0:TC],
                            M[si][:, k_, 128 * m:128 * (m + 1)],
                            xlnT[:, k_, 0:TC],
                            start=(k_ == 0), stop=(k_ == 7)),
                        w=[txln, pwar, sig_toks[cb][si]]
                        if kt == 0 else (),
                        inc=(kt == 7))
                te = p.emit("v", lambda v, si=si, m=m: v.tensor_scalar(
                    sigbuf[si][:, m, :], pps[:, 0:TC],
                    bfcol[:, si, m:m + 1], None, op0=ALU.add),
                    w=[tm], inc=True)
                pool.done(pi, te)
        return p.tok("v")

    def emit_rest(c, tsig):
        nonlocal carry_tok, free_outf, free_attdm, free_x1t, tsO
        cb = c % 2
        xb = xt[c % 3]
        qf, kf, vf = qf2[cb], kf2[cb], vf2[cb]
        outf = outf2[cb]
        sigbuf = [qf, kf, vf]
        tnrm = []
        prev_nrm = []
        for si in range(3):
            B = sigbuf[si]
            re = B[:, 0:5, :]
            im = B[:, 5:10, :]
            ta = p.emit("s", lambda s: s.activation(
                t4[0][:, :, :], re, AF.Square), w=[tsig] + prev_nrm,
                inc=True)
            tb_ = p.emit("s", lambda s: s.activation(
                t4[1][:, :, :], im, AF.Square), w=[tsig], inc=True)
            tv = p.emit("v", lambda v: v.tensor_add(
                t4[0][:, :, :], t4[0][:, :, :], t4[1][:, :, :]),
                w=[ta, tb_], inc=True)
            tl = p.emit("s", lambda s: s.activation(
                t4[1][:, :, :], t4[0][:, :, :], AF.Ln, bias=eps30[:, :]),
                w=[tv, tEps], inc=True)
            ts_ = p.emit("s", lambda s: s.activation(
                t4[0][:, :, :], t4[1][:, :, :], AF.Exp, scale=0.5),
                w=[tl], inc=True)
            tl2 = p.emit("s", lambda s: s.activation(
                t4[1][:, :, :], t4[0][:, :, :], AF.Ln, bias=epsfft[:, :]),
                w=[ts_, tEps], inc=True)
            tr_ = p.emit("s", lambda s: s.activation(
                t4[0][:, :, :], t4[1][:, :, :], AF.Exp, scale=-1.0),
                w=[tl2], inc=True)
            n1 = p.emit("v", lambda v: v.tensor_mul(re, re, t4[0][:, :, :]),
                        w=[tr_], inc=True)
            n2 = p.emit("v", lambda v: v.tensor_mul(im, im, t4[0][:, :, :]),
                        w=[tr_], inc=True)
            tnrm += [n1, n2]
            prev_nrm = [n1, n2]

        kre, kim = kf[:, 0:5, :], kf[:, 5:10, :]
        vre, vim = vf[:, 0:5, :], vf[:, 5:10, :]
        t0 = p.emit("v", lambda v: v.tensor_mul(t4[0][:, :, :], kre, vre),
                    w=tnrm, inc=True)
        t1 = p.emit("v", lambda v: v.tensor_mul(t4[1][:, :, :], kim, vim),
                    w=[], inc=True)
        t2 = p.emit("v", lambda v: v.tensor_mul(t4[2][:, :, :], kre, vim),
                    w=[], inc=True)
        tXR = p.emit("v", lambda v: v.tensor_sub(
            t4[0][:, :, :], t4[0][:, :, :], t4[1][:, :, :]), w=[t0, t1],
            inc=True)
        t3 = p.emit("v", lambda v: v.tensor_mul(t4[1][:, :, :], kim, vre),
                    w=[tXR], inc=True)
        tXI = p.emit("v", lambda v: v.tensor_add(
            t4[1][:, :, :], t4[2][:, :, :], t4[1][:, :, :]), w=[t2, t3],
            inc=True)

        tscan = None
        for j in range(NF):
            src_ = t4[0][:, j, :] if j < 5 else t4[1][:, j - 5, :]
            tscan = p.emit("v", lambda v, j=j, src_=src_: v.tensor_tensor_scan(
                kf[:, j, :], src_, zrow[:, :], carry[:, j:j + 1],
                op0=ALU.add, op1=ALU.add),
                w=[tXR, tXI, tZ, carry_tok], inc=True)
        carry_tok = p.emit("v", lambda v: v.tensor_copy(
            carry[:, :], kf[:, :, TC - 1]), w=[tscan], inc=True)

        sre, sim_ = kf[:, 0:5, :], kf[:, 5:10, :]
        qre, qim = qf[:, 0:5, :], qf[:, 5:10, :]
        u0 = p.emit("v", lambda v: v.tensor_mul(t4[2][:, :, :], sre, qre),
                    w=[carry_tok], inc=True)
        u1 = p.emit("v", lambda v: v.tensor_mul(vre, sim_, qim),
                    w=[], inc=True)
        tOR = p.emit("v", lambda v: v.tensor_add(
            outf[:, 0:5, :], t4[2][:, :, :], vre),
            w=[u0, u1, free_outf], inc=True)
        u2 = p.emit("v", lambda v: v.tensor_mul(t4[2][:, :, :], sim_, qre),
                    w=[tOR], inc=True)
        u3 = p.emit("v", lambda v: v.tensor_mul(vre, sre, qim),
                    w=[tOR], inc=True)
        tOI = p.emit("v", lambda v: v.tensor_sub(
            outf[:, 5:10, :], t4[2][:, :, :], vre),
            w=[u2, u3], inc=True)
        sig_toks[cb][0] = tOI
        sig_toks[cb][1] = tOI
        sig_toks[cb][2] = tOI

        tidft = []
        for m in range(8):
            pi, pps, pwar = pool.get()
            tm = None
            for kt in range(NF):
                tm = p.emit(
                    "t",
                    lambda t, k_=kt: t.matmul(
                        pps[:, 0:TC],
                        c_idft[:, k_, 128 * m:128 * (m + 1)],
                        outf[:, k_, 0:TC],
                        start=(k_ == 0), stop=(k_ == NF - 1)),
                    w=[tOR, tOI, pwar, tC2] if kt == 0 else (),
                    inc=(kt == NF - 1))
            te = p.emit("s", lambda s, m=m: s.copy(
                attdm[:, m, :], pps[:, 0:TC]), w=[tm, free_attdm],
                inc=True)
            pool.done(pi, te)
            tidft.append(te)
        free_outf = tidft[-1]
        tatt = tidft[-1]

        for tt in range(2):
            tadds = []
            for m in range(8):
                ti_, tps, twar = poolT.get()
                tp = mm_transpose(p, tps[:, 0:128],
                                  attdm[:, m, 128 * tt:128 * (tt + 1)],
                                  identf[:, :], w=[tatt, twar, free_x1t])
                ta = p.emit("v", lambda v, m=m, tt=tt: v.scalar_tensor_tensor(
                    x1t[:, 128 * m:128 * (m + 1)], tps[:, 0:128], 1.0,
                    xb[:, tt, 128 * m:128 * (m + 1)],
                    op0=ALU.mult, op1=ALU.add),
                    w=[tp], inc=True)
                poolT.done(ti_, ta)
                tadds.append(ta)
            tsO = p.dma("q", xout3[c, tt], x1t[:, :], sO, w=tadds)
            free_x1t = tsO
        free_attdm = p.tok("t")
        xfree[c % 3] = tsO

    pend = None
    for c in range(NCH):
        ts_ = emit_apply(c)
        if pend is not None:
            emit_rest(*pend)
        pend = (c, ts_)
    emit_rest(*pend)

    p.wait("q", [tsO])
    return tsO


def emit_l0_ffn(p, es, T, xin, xout, W, C):
    nc = p.nc
    TS = 1024
    NSEC = T // TS
    NTL = TS // 128
    NG = 16
    GE = 4
    NE = 64
    HS = 2048

    sb = lambda name, shape, dt: es.enter_context(
        nc.sbuf_tensor(name, shape, dt))

    ident = sb("f_ident", [128, 128], BF16)
    identf = sb("f_identf", [128, 128], F32)
    epsln = sb("f_epsln", [128, 1], F32)
    onesrow = sb("f_ones", [1, 128], BF16)
    rwb = sb("f_rwb", [128, 8, 64], BF16)
    rwlo = sb("f_rwlo", [128, 8, 64], BF16)
    rwf = sb("f_rwf", [128, 8, 64], F32)
    xlo = sb("f_xlo", [128, 8, 128], BF16)
    rbrow = sb("f_rbrow", [1, 64], BF16)
    b2all = sb("f_b2all", [64, 1024], BF16)
    sb2row = sb("f_sb2row", [1, 2048], BF16)
    b1rows = sb("f_b1rows", [1, 3, GE * 256], BF16)
    sb1row = sb("f_sb1row", [1, HS], BF16)
    xt = [sb("f_xt0", [128, 1024], F32), sb("f_xt1", [128, 1024], F32)]
    lnt = sb("f_lnt", [128, 1024], F32)
    stat = sb("f_stat", [128, 16], F32)
    rstat = sb("f_rstat", [128, 12], F32)
    xlnT = sb("f_xlnT", [128, 8, TS], BF16)
    cw = sb("f_cw", [128, NTL, 64], F32)
    cwb = sb("f_cwb", [128, 64], BF16)
    cwT = sb("f_cwT", [64, TS], BF16)
    acc = sb("f_acc", [128, NTL, 1024], F32)
    wpool = sb("f_wpool", [128, 3, 16384], BF16)
    wg1v = [wpool[:, b, 0:8192].rearrange(
        "p (kt e c) -> p kt e c", kt=8, e=GE) for b in range(3)]
    wg2v = [wpool[:, b, 8192:16384].rearrange(
        "p (ht e c) -> p ht e c", ht=2, e=GE) for b in range(3)]
    ws1v = [wpool[:, b, :].rearrange("p (kt c) -> p kt c", kt=8)
            for b in range(3)]
    ws2v = [wpool[:, b, :].rearrange("p (ht c) -> p ht c", ht=16)
            for b in range(3)]
    h1s2 = sb("f_h1s2", [128, 2, 1024], BF16)
    h1c2 = sb("f_h1c2", [128, 2, 1024], BF16)
    h1dm2 = sb("f_h1dm2", [128, 2, 8, 128], BF16)
    hdmS = sb("f_hdmS", [128, 2, 16, 128], BF16)
    x2t = sb("f_x2t", [128, 1024], F32)

    pAcc = [es.enter_context(nc.psum_tensor(f"f_pa{i}", [128, 512], F32))
            for i in range(2)]
    poolH = PsumPool(es, nc, "f_ph", 4, [128, 512], F32)
    poolT = PsumPool(es, nc, "f_pt", 2, [128, 128], F32)

    sC = p.dma_sem("fc")
    sCg = p.dma_sem("fcg")
    sW = [p.dma_sem("fw0"), p.dma_sem("fw1"), p.dma_sem("fw2")]
    sWS = p.dma_sem("fws")
    sB = p.dma_sem("fb")
    sX = [p.dma_sem("fx0"), p.dma_sem("fx1")]
    sXR = p.dma_sem("fxr")
    sO = p.dma_sem("fo")

    p.dma("q", ident[:, :], C["c_ident"], sC)
    tCq = p.dma("q", identf[:, :], C["c_identf"], sC)
    p.dma("g", rwb[:, :, :],
          W["router_w"].rearrange("(k p) e -> p k e", p=128), sCg)
    sRW = p.dma_sem("frw")
    tRWf = p.dma("q", rwf[:, :, :],
                 W["router_w"].rearrange("(k p) e -> p k e", p=128), sRW)
    p.dma("g", rbrow[:, :], W["router_b"][None, :], sCg)
    p.dma("g", b2all[:, :], W["b2"], sCg)
    tCg = p.dma("g", sb2row[:, :],
                W["sb2"].rearrange("a d -> (a d)")[None, :], sCg)
    tC = None
    p.emit("g", lambda g: g.memset(onesrow[:, :], 1.0), inc=True)
    tO1 = p.emit("g", lambda g: g.memset(epsln[:, :], 1e-5), inc=True)
    tRlo = p.emit("v", lambda v: v.tensor_sub(
        rwlo[:, :, :], rwf[:, :, :], rwb[:, :, :]), w=[tRWf, tCg],
        inc=True)

    xin2 = xin.rearrange("(n p) d -> n p d", p=128)
    xout2 = xout.rearrange("(n p) d -> n p d", p=128)

    pacc_free = [None] * 4
    free_h1s = [None, None]
    free_h1c = [None, None]
    free_wg = [None, None, None]
    free_xlnT = None
    free_cw = None
    free_acc = None
    free_ws = None
    tsO = None
    tX = [None, None]
    xfree = [None, None]

    for sec in range(NSEC):
        txl = []
        tcwl = []
        free_lnt = globals().get("_never", None)
        for tl in range(NTL):
            gtile = sec * NTL + tl
            cb = tl % 2
            tX[cb] = p.dma("q", xt[cb][:, :], xin2[gtile], sX[cb],
                           w=[xfree[cb]])
            tln = emit_ln_tile(p, xt[cb][:, :], lnt[:, :], stat,
                               w_src=[tX[cb], tO1], w_lnt=[free_lnt],
                               epscol=epsln[:, :])
            xfree[cb] = tln
            for m in range(8):
                ti_, tps, twar = poolT.get()
                tp = mm_transpose(p, tps[:, 0:128],
                                  lnt[:, 128 * m:128 * (m + 1)],
                                  identf[:, :], w=[tln, tCq, twar])
                te = p.emit("s", lambda s: s.copy(
                    xlnT[:, m, 128 * tl:128 * (tl + 1)], tps[:, 0:128]),
                    w=[tp], inc=True)
                poolT.done(ti_, te)
                txl.append(te)
                free_lnt = tp
            hi, hps, hwar = poolH.get()
            tm = None
            for kt in range(8):
                tm = p.emit("t", lambda t, k_=kt: t.matmul(
                    hps[:, 0:64], xlnT[:, k_, 128 * tl:128 * (tl + 1)],
                    rwb[:, k_, :], start=(k_ == 0), stop=False),
                    w=[txl[-1], tCg, hwar] if kt == 0 else (), inc=False)
            tm = p.emit("t", lambda t: t.matmul(
                hps[:, 0:64], onesrow[:, 0:128], rbrow[:, :],
                start=False, stop=True), w=[tO1], inc=True)
            tlg = p.emit("v", lambda v: v.tensor_copy(
                cw[:, tl, :], hps[:, 0:64]), w=[tm, free_cw], inc=True)
            poolH.done(hi, tlg)
            tmx = p.emit("v", lambda v: v.reduce_max(
                rstat[:, 0:1], cw[:, tl, :], axis=AX.X), w=[tlg], inc=True)
            tnm = p.emit("v", lambda v: v.tensor_scalar_mul(
                rstat[:, 1:2], rstat[:, 0:1], -1.0), w=[tmx], inc=True)
            tex = p.emit("s", lambda s: s.activation(
                cw[:, tl, :], cw[:, tl, :], AF.Exp, bias=rstat[:, 1:2]),
                w=[tnm], inc=True)
            tt8 = p.emit("v", lambda v: v.max(rstat[:, 2:10], cw[:, tl, :]),
                         w=[tex], inc=True)
            tcu = p.emit("v", lambda v: v.scalar_tensor_tensor(
                cw[:, tl, :], cw[:, tl, :], rstat[:, 9:10], cw[:, tl, :],
                op0=ALU.is_ge, op1=ALU.mult), w=[tt8], inc=True)
            ts8 = p.emit("v", lambda v: v.reduce_sum(
                rstat[:, 10:11], cw[:, tl, :], axis=AX.X), w=[tcu], inc=True)
            tr8 = p.emit("v", lambda v: v.reciprocal(
                rstat[:, 11:12], rstat[:, 10:11]), w=[ts8], inc=True)
            tcw = p.emit("v", lambda v: v.tensor_scalar_mul(
                cw[:, tl, :], cw[:, tl, :], rstat[:, 11:12]),
                w=[tr8], inc=True)
            tcb = p.emit("s", lambda s: s.copy(cwb[:, :], cw[:, tl, :]),
                         w=[tcw], inc=True)
            ti_, tps, twar = poolT.get()
            tp = mm_transpose(p, tps[0:64, 0:128], cwb[:, 0:64],
                              ident[:, :], w=[tcb, twar])
            tce = p.emit("s", lambda s: s.copy(
                cwT[:, 128 * tl:128 * (tl + 1)], tps[0:64, 0:128]),
                w=[tp], inc=True)
            poolT.done(ti_, tce)
            tcwl.append(tcw)
            tcwl.append(tce)
        txln = txl[-1]
        tcwT = tcwl[-1]
        free_xlnT = None

        tZa = p.emit("g", lambda g: g.memset(acc[:, :, :], 0.0),
                     w=[free_acc], inc=True)

        ng_tok = []
        for g in range(NG):
            gb = g % 3
            for e_ in range(GE):
                p.dma("g", wg1v[gb][:, :, e_, :],
                      W["w1"][g * GE + e_].rearrange(
                          "(kt p) c -> p kt c", p=128), sW[gb],
                      w=[free_wg[gb]] if e_ == 0 else ())
                p.dma("g", wg2v[gb][:, :, e_, :],
                      W["w2"][g * GE + e_].rearrange(
                          "(ht p) c -> p ht c", p=128), sW[gb])
            tWg = p.dma("g", b1rows[:, gb, :],
                        W["b1"][g * GE:(g + 1) * GE].rearrange(
                            "e c -> (e c)")[None, :], sW[gb])
            wg1 = wg1v[gb]
            wg2 = wg2v[gb]
            gtoks = []

            def emit_h1(tl, slot, tWg=tWg, wg1=wg1):
                hi0, hp0, hw0 = poolH.get()
                hi1, hp1, hw1 = poolH.get()
                tm = None
                for kt in range(8):
                    for ep in range(2):
                        tm = p.emit("t", lambda t, k_=kt, ep=ep: t.matmul(
                            (hp0 if ep == 0 else hp1)[:, 0:512],
                            xlnT[:, k_, 128 * tl:128 * (tl + 1)],
                            wg1[:, k_, 2 * ep:2 * ep + 2, :],
                            start=(k_ == 0), stop=False),
                            w=([txln, tWg, hw0, hw1]
                               if kt == 0 and ep == 0 else ()),
                            inc=False)
                for ep in range(2):
                    tm = p.emit("t", lambda t, ep=ep: t.matmul(
                        (hp0 if ep == 0 else hp1)[:, 0:512],
                        onesrow[:, 0:128],
                        b1rows[:, gb, 512 * ep:512 * (ep + 1)],
                        start=False, stop=True), w=[], inc=(ep == 1))
                return (tl, slot, hi0, hp0, hi1, hp1, tm)

            def emit_tail(st, g=g, wg2=wg2):
                nonlocal free_h1s, free_h1c
                tl, slot, hi0, hp0, hi1, hp1, tm = st
                tsi0 = p.emit("s", lambda s: s.activation(
                    h1s2[:, slot, 0:512], hp0[:, 0:512], AF.Silu),
                    w=[tm, free_h1s[slot]], inc=True)
                tsi1 = p.emit("s", lambda s: s.activation(
                    h1s2[:, slot, 512:1024], hp1[:, 0:512], AF.Silu),
                    w=[tm], inc=True)
                poolH.done(hi0, tsi0)
                poolH.done(hi1, tsi1)
                tcs = []
                for e_ in range(GE):
                    ge = g * GE + e_
                    tcs.append(p.emit(
                        "v", lambda v, e_=e_, ge=ge: v.tensor_scalar_mul(
                            h1c2[:, slot, 256 * e_:256 * (e_ + 1)],
                            h1s2[:, slot, 256 * e_:256 * (e_ + 1)],
                            cw[:, tl, ge:ge + 1]),
                        w=[(tsi0, tsi1)[e_ // 2], tcwl[-1],
                           free_h1c[slot]], inc=True))
                free_h1s[slot] = tcs[-1]
                tes = []
                for q4 in range(8):
                    ti_, tps, twar = poolT.get()
                    tp = mm_transpose(
                        p, tps[:, 0:128],
                        h1c2[:, slot, 128 * q4:128 * (q4 + 1)],
                        ident[:, :], w=[tcs[q4 // 2], twar])
                    te = p.emit("v", lambda v, q4=q4: v.tensor_copy(
                        h1dm2[:, slot, q4, :], tps[:, 0:128]),
                        w=[tp], inc=True)
                    poolT.done(ti_, te)
                    tes.append(te)
                free_h1c[slot] = tp
                h2last = None
                for e_ in range(GE):
                    for ht in range(2):
                        for bank in range(2):
                            h2last = p.emit(
                                "t",
                                lambda t, e_=e_, ht=ht, bank=bank: t.matmul(
                                    pAcc[bank][:, 0:512],
                                    h1dm2[:, slot, 2 * e_ + ht, :],
                                    wg2[:, ht, e_,
                                        512 * bank:512 * (bank + 1)],
                                    start=(e_ == 0 and ht == 0),
                                    stop=False,
                                    skip_group_check=True),
                                w=([tes[-1], pacc_free[0], pacc_free[1]]
                                   if e_ == 0 and ht == 0 else ()),
                                inc=(e_ == GE - 1 and ht == 1 and bank == 1))
                for bank in range(2):
                    tev = p.emit("v", lambda v, bank=bank:
                                 v.scalar_tensor_tensor(
                        acc[:, tl, 512 * bank:512 * (bank + 1)],
                        pAcc[bank][:, 0:512], 1.0,
                        acc[:, tl, 512 * bank:512 * (bank + 1)],
                        op0=ALU.mult, op1=ALU.add),
                        w=[h2last, tZa], inc=True)
                    pacc_free[bank] = tev
                    gtoks.append(tev)

            pending = None
            for tl in range(NTL):
                st = emit_h1(tl, tl % 2)
                if pending is not None:
                    emit_tail(pending)
                pending = st
            emit_tail(pending)
            ng_tok.append(gtoks[-1])
            free_wg[gb] = gtoks[-1]

        sh_tok = None
        sb1_free = None
        for se in range(2):
            b1_ = (NG + 2 * se) % 3
            b2_ = (NG + 2 * se + 1) % 3
            tS1 = p.dma("g", ws1v[b1_][:, :, :],
                        W["sw1"][se].rearrange("(kt p) c -> p kt c", p=128),
                        sW[b1_], w=[free_wg[b1_]])
            tS2 = p.dma("g", ws2v[b2_][:, :, :],
                        W["sw2"][se].rearrange("(ht p) c -> p ht c", p=128),
                        sW[b2_], w=[free_wg[b2_]])
            tSB = p.dma("g", sb1row[:, :], W["sb1"][se][None, :], sW[b1_],
                        w=[sb1_free] if sb1_free else [])
            tS1 = tSB
            ws1 = ws1v[b1_]
            ws2 = ws2v[b2_]
            shtoks = []

            def emit_h1sh(tl, hc2, slot, tS1=tS1, ws1=ws1, se=se,
                          tSB=tSB):
                hi0, hp0, hw0 = poolH.get()
                hi1, hp1, hw1 = poolH.get()
                tm = None
                for kt in range(8):
                    for hh in range(2):
                        hc = 2 * hc2 + hh
                        tm = p.emit("t", lambda t, k_=kt, hh=hh, hc=hc:
                                    t.matmul(
                            (hp0 if hh == 0 else hp1)[:, 0:512],
                            xlnT[:, k_, 128 * tl:128 * (tl + 1)],
                            ws1[:, k_, 512 * hc:512 * (hc + 1)],
                            start=(k_ == 0), stop=False),
                            w=([txln, tSB, hw0, hw1]
                               if kt == 0 and hh == 0 else ()),
                            inc=False)
                for hh in range(2):
                    hc = 2 * hc2 + hh
                    tm = p.emit("t", lambda t, hh=hh, hc=hc: t.matmul(
                        (hp0 if hh == 0 else hp1)[:, 0:512],
                        onesrow[:, 0:128],
                        sb1row[:, 512 * hc:512 * (hc + 1)],
                        start=False, stop=True), w=[], inc=(hh == 1))
                return (tl, hc2, slot, hi0, hp0, hi1, hp1, tm)

            def emit_tailsh(st, ws2=ws2, tS2=tS2):
                nonlocal free_h1c
                tl, hc2, slot, hi0, hp0, hi1, hp1, tm = st
                tsi0 = p.emit("s", lambda s: s.activation(
                    h1c2[:, slot, 0:512], hp0[:, 0:512], AF.Silu),
                    w=[tm, free_h1c[slot]], inc=True)
                tsi1 = p.emit("s", lambda s: s.activation(
                    h1c2[:, slot, 512:1024], hp1[:, 0:512], AF.Silu),
                    w=[tm], inc=True)
                poolH.done(hi0, tsi0)
                poolH.done(hi1, tsi1)
                tes = []
                for q4 in range(8):
                    ti_, tps, twar = poolT.get()
                    tp = mm_transpose(
                        p, tps[:, 0:128],
                        h1c2[:, slot, 128 * q4:128 * (q4 + 1)],
                        ident[:, :],
                        w=[(tsi0 if q4 < 4 else tsi1), twar])
                    te = p.emit("v", lambda v, q4=q4: v.tensor_copy(
                        hdmS[:, slot, q4, :], tps[:, 0:128]),
                        w=[tp], inc=True)
                    poolT.done(ti_, te)
                    tes.append(te)
                free_h1c[slot] = tp
                h2last = None
                for ht8 in range(8):
                    ht = 8 * hc2 + ht8
                    for bank in range(2):
                        h2last = p.emit(
                            "t", lambda t, ht=ht, ht8=ht8, bank=bank:
                            t.matmul(
                                pAcc[bank][:, 0:512],
                                hdmS[:, slot, ht8, :],
                                ws2[:, ht, 512 * bank:512 * (bank + 1)],
                                start=(ht == 0), stop=False,
                                skip_group_check=True),
                            w=([tes[-1], tS2, pacc_free[0], pacc_free[1]]
                               if ht8 == 0 else ()),
                            inc=(ht8 == 7 and bank == 1))
                if hc2 == 1:
                    for bank in range(2):
                        tev = p.emit("v", lambda v, bank=bank:
                                     v.scalar_tensor_tensor(
                            acc[:, tl, 512 * bank:512 * (bank + 1)],
                            pAcc[bank][:, 0:512], 1.0,
                            acc[:, tl, 512 * bank:512 * (bank + 1)],
                            op0=ALU.mult, op1=ALU.add),
                            w=[h2last, tZa], inc=True)
                        pacc_free[bank] = tev
                        shtoks.append(tev)

            pending = None
            for tl in range(NTL):
                for hc2 in range(2):
                    st = emit_h1sh(tl, hc2, (2 * tl + hc2) % 2)
                    if pending is not None:
                        emit_tailsh(pending)
                    pending = st
            emit_tailsh(pending)
            sh_tok = shtoks[-1]
            sb1_free = sh_tok
            ng_tok.append(sh_tok)
            free_wg[b1_] = sh_tok
            free_wg[b2_] = sh_tok

        for tl in range(NTL):
            gtile = sec * NTL + tl
            cb = tl % 2
            tXR_ = p.dma("q", xt[cb][:, :], xin2[gtile], sXR,
                         w=[xfree[cb]])
            for bank in range(2):
                hi, hps, hwar = poolH.get()
                tm = p.emit("t", lambda t: t.matmul(
                    hps[:, 0:512], cwT[:, 128 * tl:128 * (tl + 1)],
                    b2all[:, 512 * bank:512 * (bank + 1)],
                    start=True, stop=False),
                    w=[tcwT, hwar], inc=False)
                tm = p.emit("t", lambda t: t.matmul(
                    hps[:, 0:512], onesrow[0:1, 0:128],
                    sb2row[0:1, 512 * bank:512 * (bank + 1)],
                    start=False, stop=False), w=[], inc=False)
                tm = p.emit("t", lambda t: t.matmul(
                    hps[:, 0:512], onesrow[0:1, 0:128],
                    sb2row[0:1, 1024 + 512 * bank:1024 + 512 * (bank + 1)],
                    start=False, stop=True), w=[], inc=True)
                tv1 = p.emit("v", lambda v: v.scalar_tensor_tensor(
                    x2t[:, 512 * bank:512 * (bank + 1)], hps[:, 0:512],
                    1.0, acc[:, tl, 512 * bank:512 * (bank + 1)],
                    op0=ALU.mult, op1=ALU.add),
                    w=[tm, ng_tok[-1]], inc=True)
                poolH.done(hi, tv1)
                tv2 = p.emit("v", lambda v: v.tensor_add(
                    x2t[:, 512 * bank:512 * (bank + 1)],
                    x2t[:, 512 * bank:512 * (bank + 1)],
                    xt[cb][:, 512 * bank:512 * (bank + 1)]),
                    w=[tv1, tXR_], inc=True)
            tsO = p.dma("q", xout2[gtile], x2t[:, :], sO,
                        w=[p.tok("v")])
            xfree[cb] = tsO
            p.wait("v", [tsO])
        free_acc = p.tok("v")
        free_cw = p.tok("v")
        free_xlnT = p.tok("t")

    p.wait("q", [tsO])
    return tsO


def build_l0_ffn_program(T):
    nc = bass.Bass("TRN2", target_bir_lowering=False)
    xin = nc.dram_tensor("xin", [T, D], F32, kind="ExternalInput").ap()
    W = {
        "router_w": nc.dram_tensor("router_w", [D, 64], F32,
                                   kind="ExternalInput").ap(),
        "router_b": nc.dram_tensor("router_b", [64], F32,
                                   kind="ExternalInput").ap(),
        "w1": nc.dram_tensor("w1", [64, D, 256], F32,
                             kind="ExternalInput").ap(),
        "b1": nc.dram_tensor("b1", [64, 256], F32,
                             kind="ExternalInput").ap(),
        "w2": nc.dram_tensor("w2", [64, 256, D], F32,
                             kind="ExternalInput").ap(),
        "b2": nc.dram_tensor("b2", [64, D], F32,
                             kind="ExternalInput").ap(),
        "sw1": nc.dram_tensor("sw1", [2, D, 2048], F32,
                              kind="ExternalInput").ap(),
        "sb1": nc.dram_tensor("sb1", [2, 2048], F32,
                              kind="ExternalInput").ap(),
        "sw2": nc.dram_tensor("sw2", [2, 2048, D], F32,
                              kind="ExternalInput").ap(),
        "sb2": nc.dram_tensor("sb2", [2, D], F32,
                              kind="ExternalInput").ap(),
    }
    C = {
        "c_ident": nc.dram_tensor("c_ident", [128, 128], BF16,
                                  kind="ExternalInput").ap(),
        "c_identf": nc.dram_tensor("c_identf", [128, 128], F32,
                                   kind="ExternalInput").ap(),
    }
    xout = nc.dram_tensor("xout", [T, D], F32, kind="ExternalOutput").ap()
    with contextlib.ExitStack() as es:
        p = Prog(nc, es)
        emit_l0_ffn(p, es, T, xin, xout, W, C)
    return nc


def emit_quant_scales(p, nb, qb, w=(), b0=0):
    tc = p.emit("v", lambda v: v.tensor_scalar(
        qb["bm"][:, b0:b0 + nb], qb["bm"][:, b0:b0 + nb], 1.0 / 6.0, 1e-12,
        op0=ALU.mult, op1=ALU.max), w=list(w), inc=True)
    td = p.emit("s", lambda s: s.activation(
        qb["rs"][:, b0:b0 + nb], qb["bm"][:, b0:b0 + nb], AF.Ln),
        w=[tc, qb.get("war_rs")], inc=True)
    te = p.emit("s", lambda s: s.activation(
        qb["rs"][:, b0:b0 + nb], qb["rs"][:, b0:b0 + nb], AF.Exp,
        scale=-1.0),
        w=[td], inc=True)
    return te


def emit_quant_absmax(p, src_ap, nb, b0, qb, w_src=()):
    F = nb * 16
    s3 = qb["s3"]
    ta = p.emit("s", lambda s: s.activation(
        s3[:, 0:F], src_ap, AF.Abs),
        w=list(w_src) + [qb.get("war_s3")], inc=True)
    tb = p.emit("v", lambda v: v.tensor_reduce(
        qb["bm"][:, b0:b0 + nb],
        s3[:, 0:F].rearrange("p (b s) -> p b s", s=16),
        axis=AX.X, op=ALU.max), w=[ta, qb.get("war_bm")], inc=True)
    qb["war_s3"] = tb
    return ta, tb


def emit_quant_ladder(p, dst_ap, src_ap, nb, b0, qb, trs, w_src=(),
                      w_dst=(), ve="v"):
    F = nb * 16
    s1, s2, s3, bm, rs = (qb[k] for k in ("s1", "s2", "s3", "bm", "rs"))
    ta = p.emit("s", lambda s: s.activation(
        s3[:, 0:F], src_ap, AF.Abs),
        w=list(w_src) + [qb.get("war_s3")], inc=True)
    rsb = rs[:, b0:b0 + nb, None].broadcast_to([128, nb, 16])
    tf = p.emit(ve, lambda v: v.tensor_tensor(
        s3[:, 0:F].rearrange("p (b s) -> p b s", s=16),
        s3[:, 0:F].rearrange("p (b s) -> p b s", s=16), rsb, op=ALU.mult),
        w=[ta, trs], inc=True)
    xn = s3[:, 0:F]
    t1 = p.emit(ve, lambda v: v.tensor_scalar(
        s1[:, 0:F], xn, 0.25, None, op0=ALU.is_gt),
        w=[tf, qb.get("war_s1")], inc=True)
    for mid in (0.75, 1.25, 1.75):
        t1 = p.emit(ve, lambda v, mid=mid: v.scalar_tensor_tensor(
            s1[:, 0:F], xn, mid, s1[:, 0:F], op0=ALU.is_gt, op1=ALU.add),
            w=[t1], inc=True)
    t2 = p.emit(ve, lambda v: v.tensor_scalar(
        s2[:, 0:F], xn, 2.5, None, op0=ALU.is_gt),
        w=[tf, qb.get("war_s2")], inc=True)
    t2 = p.emit(ve, lambda v: v.scalar_tensor_tensor(
        s2[:, 0:F], xn, 3.5, s2[:, 0:F], op0=ALU.is_gt, op1=ALU.add),
        w=[t2], inc=True)
    t3 = p.emit(ve, lambda v: v.scalar_tensor_tensor(
        s2[:, 0:F], xn, 5.0, s2[:, 0:F], op0=ALU.is_gt, op1=ALU.add),
        w=[t2], inc=True)
    t3b = p.emit(ve, lambda v: v.scalar_tensor_tensor(
        s2[:, 0:F], xn, 5.0, s2[:, 0:F], op0=ALU.is_gt, op1=ALU.add),
        w=[t3], inc=True)
    tq = p.emit(ve, lambda v: v.scalar_tensor_tensor(
        s1[:, 0:F], s1[:, 0:F], 0.5, s2[:, 0:F], op0=ALU.mult, op1=ALU.add),
        w=[t1, t3b], inc=True)
    tsg = p.emit("s", lambda s: s.activation(s3[:, 0:F], src_ap, AF.Sign),
                 w=[tq], inc=True)
    bmb = bm[:, b0:b0 + nb, None].broadcast_to([128, nb, 16])
    tqs = p.emit(ve, lambda v: v.tensor_tensor(
        s1[:, 0:F].rearrange("p (b s) -> p b s", s=16),
        s1[:, 0:F].rearrange("p (b s) -> p b s", s=16), bmb, op=ALU.mult),
        w=[tq], inc=True)
    tout = p.emit(ve, lambda v: v.tensor_tensor(
        dst_ap, s1[:, 0:F], s3[:, 0:F], op=ALU.mult),
        w=[tqs, tsg] + list(w_dst), inc=True)
    qb["war_s1"] = tout
    qb["war_s2"] = tq
    qb["war_s3"] = tout
    qb["war_bm"] = tqs
    qb["war_rs"] = tf
    return tout


def emit_quant_chunk(p, dst_ap, src_ap, nb, qb, w_src=(), w_dst=()):
    F = nb * 16
    s1, s2, s3, bm, rs = (qb[k] for k in ("s1", "s2", "s3", "bm", "rs"))
    src3 = src_ap.rearrange("p (b s) -> p b s", s=16)
    ta = p.emit("s", lambda s: s.activation(
        s3[:, 0:F], src_ap, AF.Abs),
        w=list(w_src) + [qb.get("war_s3")], inc=True)
    tb = p.emit("v", lambda v: v.tensor_reduce(
        bm[:, 0:nb], s3[:, 0:F].rearrange("p (b s) -> p b s", s=16),
        axis=AX.X, op=ALU.max), w=[ta, qb.get("war_bm")], inc=True)
    tc = p.emit("v", lambda v: v.tensor_scalar(
        bm[:, 0:nb], bm[:, 0:nb], 1.0 / 6.0, 1e-12,
        op0=ALU.mult, op1=ALU.max), w=[tb], inc=True)
    td = p.emit("s", lambda s: s.activation(rs[:, 0:nb], bm[:, 0:nb], AF.Ln),
                w=[tc, qb.get("war_rs")], inc=True)
    te = p.emit("s", lambda s: s.activation(rs[:, 0:nb], rs[:, 0:nb],
                                            AF.Exp, scale=-1.0),
                w=[td], inc=True)
    rsb = rs[:, 0:nb, None].broadcast_to([128, nb, 16])
    tf = p.emit("v", lambda v: v.tensor_tensor(
        s3[:, 0:F].rearrange("p (b s) -> p b s", s=16),
        s3[:, 0:F].rearrange("p (b s) -> p b s", s=16), rsb, op=ALU.mult),
        w=[te], inc=True)
    xn = s3[:, 0:F]
    t1 = p.emit("v", lambda v: v.tensor_scalar(
        s1[:, 0:F], xn, 0.25, None, op0=ALU.is_gt),
        w=[tf, qb.get("war_s1")], inc=True)
    for mid in (0.75, 1.25, 1.75):
        t1 = p.emit("v", lambda v, mid=mid: v.scalar_tensor_tensor(
            s1[:, 0:F], xn, mid, s1[:, 0:F], op0=ALU.is_gt, op1=ALU.add),
            w=[t1], inc=True)
    t2 = p.emit("v", lambda v: v.tensor_scalar(
        s2[:, 0:F], xn, 2.5, None, op0=ALU.is_gt),
        w=[tf, qb.get("war_s2")], inc=True)
    t2 = p.emit("v", lambda v: v.scalar_tensor_tensor(
        s2[:, 0:F], xn, 3.5, s2[:, 0:F], op0=ALU.is_gt, op1=ALU.add),
        w=[t2], inc=True)
    t3 = p.emit("v", lambda v: v.scalar_tensor_tensor(
        s2[:, 0:F], xn, 5.0, s2[:, 0:F], op0=ALU.is_gt, op1=ALU.add),
        w=[t2], inc=True)
    t3b = p.emit("v", lambda v: v.scalar_tensor_tensor(
        s2[:, 0:F], xn, 5.0, s2[:, 0:F], op0=ALU.is_gt, op1=ALU.add),
        w=[t3], inc=True)
    tq = p.emit("v", lambda v: v.scalar_tensor_tensor(
        s1[:, 0:F], s1[:, 0:F], 0.5, s2[:, 0:F], op0=ALU.mult, op1=ALU.add),
        w=[t1, t3b], inc=True)
    tsg = p.emit("s", lambda s: s.activation(s3[:, 0:F], src_ap, AF.Sign),
                 w=[tq], inc=True)
    bmb = bm[:, 0:nb, None].broadcast_to([128, nb, 16])
    tqs = p.emit(ve, lambda v: v.tensor_tensor(
        s1[:, 0:F].rearrange("p (b s) -> p b s", s=16),
        s1[:, 0:F].rearrange("p (b s) -> p b s", s=16), bmb, op=ALU.mult),
        w=[tq], inc=True)
    tout = p.emit(ve, lambda v: v.tensor_tensor(
        dst_ap, s1[:, 0:F], s3[:, 0:F], op=ALU.mult),
        w=[tqs, tsg] + list(w_dst), inc=True)
    qb["war_s1"] = tout
    qb["war_s2"] = tq
    qb["war_s3"] = tout
    qb["war_bm"] = tqs
    qb["war_rs"] = tf
    return tout


def emit_l1_ffn(p, es, T, xin, xout, W, C):
    nc = p.nc
    NTL = T // 128
    II = 2816
    NIT = 22
    NIC = 6

    sb = lambda name, shape, dt: es.enter_context(
        nc.sbuf_tensor(name, shape, dt))

    ident = sb("u_ident", [128, 128], BF16)
    identf = sb("u_identf", [128, 128], F32)
    epsln = sb("u_epsln", [128, 1], F32)
    wgT = sb("u_wgT", [128, 8, II], BF16)
    wuT = sb("u_wuT", [128, 8, II], BF16)
    wdT = sb("u_wdT", [128, NIT, 1024], BF16)
    xt = [sb("u_xt0", [128, 1024], F32), sb("u_xt1", [128, 1024], F32)]
    lnt = sb("u_lnt", [128, 1024], F32)
    stat = sb("u_stat", [128, 16], F32)
    xq = sb("u_xq", [128, 1024], BF16)
    xqT = sb("u_xqT", [128, 8, 128], BF16)
    hfull = sb("u_hfull", [128, II], F32)
    hq = sb("u_hq", [128, II], BF16)
    hqT = sb("u_hqT", [128, NIT, 128], BF16)
    x4t = sb("u_x4t", [128, 1024], F32)
    qs1 = sb("u_qs1", [128, 512], F32)
    qs2 = sb("u_qs2", [128, 512], F32)
    qs3 = sb("u_qs3", [128, 512], F32)
    qbm = sb("u_qbm", [128, 240], F32)
    qrs = sb("u_qrs", [128, 240], F32)
    qb = {"s1": qs1, "s2": qs2, "s3": qs3, "bm": qbm, "rs": qrs}
    qx1 = sb("u_qx1", [128, 512], F32)
    qx2 = sb("u_qx2", [128, 512], F32)
    qx3 = sb("u_qx3", [128, 512], F32)
    qbx = {"s1": qx1, "s2": qx2, "s3": qx3, "bm": qbm, "rs": qrs}

    poolG = PsumPool(es, nc, "u_pg", 4, [128, 512], F32)
    poolT = PsumPool(es, nc, "u_pt", 2, [128, 128], F32)
    poolD = PsumPool(es, nc, "u_pd", 2, [128, 512], F32)

    sC = p.dma_sem("uc")
    sW = p.dma_sem("uw")
    sX = [p.dma_sem("ux0"), p.dma_sem("ux1")]
    sO = p.dma_sem("uo")

    p.dma("q", ident[:, :], C["c_ident"], sC)
    tCq = p.dma("q", identf[:, :], C["c_identf"], sC)
    tO1 = p.emit("g", lambda g: g.memset(epsln[:, :], 1e-5), inc=True)

    es2 = contextlib.ExitStack()
    stage = es2.enter_context(nc.sbuf_tensor("u_stage", [128, 3072], F32))
    qnat = es2.enter_context(nc.sbuf_tensor("u_qnat", [128, 3072], BF16))
    free_stage = None
    free_qnat = None
    for wi, (wname, dstT) in enumerate(
            [("w_gate", wgT), ("w_up", wuT), ("w_down", wdT)]):
        if wname == "w_down":
            npt, fw = 8, II
        else:
            npt, fw = NIT, 1024
        done = 0
        while done < npt:
            nstep = min(3 if fw == 1024 else 1, npt - done)
            stv = stage[:, 0:nstep * fw].rearrange("p (j c) -> p j c", c=fw)
            qnv = qnat[:, 0:nstep * fw].rearrange("p (j c) -> p j c", c=fw)
            tst = p.dma("g", stv,
                        W[wname].rearrange("(kt p) c -> p kt c", p=128)[
                            :, done:done + nstep, :], sW,
                        w=[free_stage])
            tql = None
            for j in range(nstep):
                for c0 in range(0, fw, 512):
                    cw = min(512, fw - c0)
                    _, tbm = emit_quant_absmax(
                        p, stv[:, j, c0:c0 + cw], cw // 16, 0, qb,
                        w_src=[tst])
                    trs_ = emit_quant_scales(p, cw // 16, qb, w=[tbm])
                    tql = emit_quant_ladder(
                        p, qnv[:, j, c0:c0 + cw], stv[:, j, c0:c0 + cw],
                        cw // 16, 0, qb, trs_, w_src=[tst],
                        w_dst=[free_qnat])
            free_stage = tql
            tes = []
            for j in range(nstep):
                for c0 in range(0, fw, 128):
                    ti_, tps, twar = poolT.get()
                    tp = mm_transpose(p, tps[:, 0:128],
                                      qnv[:, j, c0:c0 + 128],
                                      ident[:, :], w=[tql, twar, tCq])
                    it = done + j
                    te = p.emit("s", lambda s, it=it, c0=c0: s.copy(
                        dstT[:, (c0 // 128), it * 128:(it + 1) * 128]
                        if dstT is not wdT else
                        dstT[:, (c0 // 128), it * 128:(it + 1) * 128],
                        tps[:, 0:128]), w=[tp], inc=True)
                    poolT.done(ti_, te)
                    tes.append(te)
            free_qnat = tes[-1]
            done += nstep
    tWq = p.tok("s")
    es2.close()

    xin2 = xin.rearrange("(n p) d -> n p d", p=128)
    xout2 = xout.rearrange("(n p) d -> n p d", p=128)
    xfree = [None, None]
    fr_state = {"lnt": None, "xq": None}
    free_xqT = None
    free_h = None
    free_hq = None
    free_hqT = None
    free_x4t = None
    tsO = None

    def emit_l1f_front(tl):
        cb = tl % 2
        tXc = p.dma("q", xt[cb][:, :], xin2[tl], sX[cb], w=[xfree[cb]])
        tln = emit_ln_tile(p, xt[cb][:, :], lnt[:, :], stat,
                           w_src=[tXc, tO1], w_lnt=[fr_state["lnt"]],
                           epscol=epsln[:, :])
        tb0 = emit_quant_absmax(p, lnt[:, 0:512], 32, 176, qbx,
                                w_src=[tln])[1]
        tb1 = emit_quant_absmax(p, lnt[:, 512:1024], 32, 176 + 32, qbx,
                                w_src=[tln])[1]
        trsx = emit_quant_scales(p, 64, qbx, w=[tb0, tb1], b0=176)
        tq = None
        for ci, c0 in enumerate((0, 512)):
            tq = emit_quant_ladder(p, xq[:, c0:c0 + 512],
                                   lnt[:, c0:c0 + 512], 32, 176 + 32 * ci,
                                   qbx, trsx, w_src=[tln],
                                   w_dst=[fr_state["xq"]])
        fr_state["lnt"] = tq
        return tq

    pend_out = [None]
    out_state = {"x4t": None, "tsO": None}

    def emit_l1f_out():
        if pend_out[0] is None:
            return
        tl_, cb_, banks = pend_out[0]
        pend_out[0] = None
        dtoks = []
        for (di, dps, tmd, bank) in banks:
            tv = p.emit("v", lambda v, dps=dps, bank=bank, cb_=cb_:
                        v.scalar_tensor_tensor(
                x4t[:, 512 * bank:512 * (bank + 1)], dps[:, 0:512], 1.0,
                xt[cb_][:, 512 * bank:512 * (bank + 1)],
                op0=ALU.mult, op1=ALU.add),
                w=[tmd, out_state["x4t"]], inc=True)
            poolD.done(di, tv)
            dtoks.append(tv)
        tsO = p.dma("q", xout2[tl_], x4t[:, :], sO, w=dtoks)
        out_state["x4t"] = tsO
        out_state["tsO"] = tsO
        xfree[cb_] = tsO

    tq_next = emit_l1f_front(0)
    for tl in range(NTL):
        cb = tl % 2
        tq = tq_next
        tes = []
        for m in range(8):
            ti_, tps, twar = poolT.get()
            tp = mm_transpose(p, tps[:, 0:128], xq[:, 128 * m:128 * (m + 1)],
                              ident[:, :], w=[tq, twar, free_xqT])
            te = p.emit("s", lambda s, m=m: s.copy(
                xqT[:, m, :], tps[:, 0:128]), w=[tp], inc=True)
            poolT.done(ti_, te)
            tes.append(te)
        fr_state["xq"] = tes[-1]
        txq = tes[-1]

        gu = []
        free_hf = free_h

        def emit_gu(ic):
            c0 = 512 * ic
            cw_ = min(512, II - c0)
            gi, gps, gwar = poolG.get()
            tmg = None
            for kt in range(8):
                tmg = p.emit("t", lambda t, k_=kt: t.matmul(
                    gps[:, 0:cw_], xqT[:, k_, :], wgT[:, k_, c0:c0 + cw_],
                    start=(k_ == 0), stop=(k_ == 7)),
                    w=[txq, tWq, gwar] if kt == 0 else (), inc=(kt == 7))
            ui, ups, uwar = poolG.get()
            tmu = None
            for kt in range(8):
                tmu = p.emit("t", lambda t, k_=kt: t.matmul(
                    ups[:, 0:cw_], xqT[:, k_, :], wuT[:, k_, c0:c0 + cw_],
                    start=(k_ == 0), stop=(k_ == 7)),
                    w=[uwar] if kt == 0 else (), inc=(kt == 7))
            return (ic, c0, cw_, gi, gps, tmg, ui, ups, tmu)

        def emit_gu_tail(st):
            nonlocal free_hf
            ic, c0, cw_, gi, gps, tmg, ui, ups, tmu = st
            tsi = p.emit("s", lambda s: s.activation(
                hfull[:, c0:c0 + cw_], gps[:, 0:cw_], AF.Silu),
                w=[tmg, free_hf], inc=True)
            poolG.done(gi, tsi)
            tmul = p.emit("v", lambda v: v.tensor_tensor(
                hfull[:, c0:c0 + cw_], hfull[:, c0:c0 + cw_],
                ups[:, 0:cw_], op=ALU.mult),
                w=[tsi, tmu], inc=True)
            poolG.done(ui, tmul)
            _, tbm = emit_quant_absmax(p, hfull[:, c0:c0 + cw_], cw_ // 16,
                                       32 * ic, qb, w_src=[tmul])
            return tmul, tbm

        pend_gu = None
        gutails = []
        for ic in range(NIC):
            st = emit_gu(ic)
            if pend_gu is not None:
                gutails.append(emit_gu_tail(pend_gu))
            pend_gu = st
        gutails.append(emit_gu_tail(pend_gu))
        if tl + 1 < NTL:
            tq_next = emit_l1f_front(tl + 1)
        trs = emit_quant_scales(p, II // 16, qb,
                                w=[t for (_, t) in gutails])
        thqT = []
        tqh = None
        for ic in range(NIC):
            c0 = 512 * ic
            cw_ = min(512, II - c0)
            tqh = emit_quant_ladder(p, hq[:, c0:c0 + cw_],
                                    hfull[:, c0:c0 + cw_], cw_ // 16,
                                    32 * ic, qb, trs,
                                    w_src=[gutails[ic][0]],
                                    w_dst=[free_hq])
            for q4 in range(cw_ // 128):
                ti_, tps, twar = poolT.get()
                tp = mm_transpose(p, tps[:, 0:128],
                                  hq[:, c0 + 128 * q4:c0 + 128 * (q4 + 1)],
                                  ident[:, :], w=[tqh, twar, free_hqT])
                it = 4 * ic + q4
                te = p.emit("v", lambda v, it=it: v.tensor_copy(
                    hqT[:, it, :], tps[:, 0:128]), w=[tp], inc=True)
                poolT.done(ti_, te)
                thqT.append(te)
            free_hq = thqT[-1]
        free_hqT = None
        thq = thqT[-1]
        free_h = tqh

        banks = []
        for bank in range(2):
            di, dps, dwar = poolD.get()
            tmd = None
            for it in range(NIT):
                tmd = p.emit("t", lambda t, it=it: t.matmul(
                    dps[:, 0:512], hqT[:, it, :],
                    wdT[:, it, 512 * bank:512 * (bank + 1)],
                    start=(it == 0), stop=(it == NIT - 1)),
                    w=[thq, dwar] if it == 0 else (), inc=(it == NIT - 1))
            banks.append((di, dps, tmd, bank))
        free_hqT = p.tok("t")
        pend_out[0] = (tl, cb, banks)
        emit_l1f_out()

    emit_l1f_out()
    p.wait("q", [out_state["tsO"]])
    return out_state["tsO"]


def build_l1_ffn_program(T):
    nc = bass.Bass("TRN2", target_bir_lowering=False)
    xin = nc.dram_tensor("xin", [T, D], F32, kind="ExternalInput").ap()
    W = {
        "w_gate": nc.dram_tensor("w_gate", [2816, D], F32,
                                 kind="ExternalInput").ap(),
        "w_up": nc.dram_tensor("w_up", [2816, D], F32,
                               kind="ExternalInput").ap(),
        "w_down": nc.dram_tensor("w_down", [D, 2816], F32,
                                 kind="ExternalInput").ap(),
    }
    C = {
        "c_ident": nc.dram_tensor("c_ident", [128, 128], BF16,
                                  kind="ExternalInput").ap(),
        "c_identf": nc.dram_tensor("c_identf", [128, 128], F32,
                                   kind="ExternalInput").ap(),
    }
    xout = nc.dram_tensor("xout", [T, D], F32, kind="ExternalOutput").ap()
    with contextlib.ExitStack() as es:
        p = Prog(nc, es)
        emit_l1_ffn(p, es, T, xin, xout, W, C)
    return nc


def emit_l1_attn(p, es, T, xin, xout, W, C, stage=99):
    nc = p.nc
    TS = 256
    NSEC = T // TS
    NCH = 2

    sb = lambda name, shape, dt: es.enter_context(
        nc.sbuf_tensor(name, shape, dt))

    ident = sb("g_ident", [128, 128], BF16)
    identf = sb("g_identf", [128, 128], F32)
    epsln = sb("g_epsln", [128, 1], F32)
    onesrow = sb("g_ones", [1, 128], BF16)
    zrow1 = sb("g_zrow1", [1, 128], BF16)
    zr128 = sb("g_zr128", [128, 128], F32)
    mtril = sb("g_mtril", [128, 128], F32)
    m16off = sb("g_m16off", [128, 128], F32)
    mpair = sb("g_mpair", [128, 128], F32)
    mcols = sb("g_mcols", [128, 2], F32)
    kwT = sb("g_kwT", [128, 8, 1024], BF16)
    vqTq = sb("g_vqTq", [128, 8, 1024], BF16)
    vqTv = sb("g_vqTv", [128, 8, 1024], BF16)
    owT = sb("g_owT", [128, 8, 1024], BF16)
    kbcol = sb("g_kbcol", [128, 8], F32)
    obrow = sb("g_obrow", [1, 1024], BF16)
    xt = [sb("g_xt0", [128, 1024], F32), sb("g_xt1", [128, 1024], F32)]
    lnt = sb("g_lnt", [128, 1024], F32)
    stat = sb("g_stat", [128, 16], F32)
    xlnT = sb("g_xlnT", [128, 8, TS], BF16)
    ksec = sb("g_ksec", [128, 8, TS], F32)
    logk = sb("g_logk", [128, 8, TS], F32)
    omk = sb("g_omk", [128, 8, TS], BF16)
    qdm = sb("g_qdm", [128, 8, TS], BF16)
    vtm = sb("g_vtm", [128, NCH, 1024], BF16)
    Lc = sb("g_L", [128, 8, TS], F32)
    htm = sb("g_htm", [128, NCH, 1024], BF16)
    hm = sb("g_hm", [128, 8, TS], BF16)
    x3t = sb("g_x3t", [128, 1024], F32)
    argb = sb("g_arg", [128, 8, 128], F32)
    ebuf = sb("g_ebuf", [128, 8, 128], BF16)
    qh16 = sb("g_qh16", [128, 8, 128], BF16)
    qh32 = sb("g_qh32", [128, 8, 128], BF16)
    qh64 = sb("g_qh64", [128, 8, 128], BF16)
    qt128d = [sb("g_qt128_0", [128, 8, 128], BF16), None]
    kh16d = [sb("g_kh16_0", [128, 8, 128], BF16), None]
    kh32d = [sb("g_kh32_0", [128, 8, 128], BF16), None]
    kh64d = [sb("g_kh64_0", [128, 8, 128], BF16), None]
    khstd = [sb("g_khst_0", [128, 8, 128], BF16), None]
    khrd = [sb("g_khr_0", [128, 8, 128], BF16), None]
    qzd = [[sb(f"g_qz{i}_0", [128, 8, 128], BF16) for i in range(6)], None]
    khtm = sb("g_khtm", [128, 8, 128], BF16)
    base16 = sb("g_b16", [128, 8, 8], F32)
    base32 = sb("g_b32", [128, 8, 4], F32)
    base64 = sb("g_b64", [128, 8, 2], F32)
    fac = sb("g_fac", [128, 8, 8], F32)
    fac2 = sb("g_fac2", [128, 8, 8], F32)
    pend = sb("g_pend", [128, 16], F32)
    Sblk = sb("g_Sblk", [128, 8, 128], F32)
    Sbf = sb("g_Sbf", [128, 8, 128], BF16)
    stmp = sb("g_stmp", [128, 128], F32)
    abuf = [sb(f"g_ab{i}", [128, 128], BF16) for i in range(6)]

    poolP = PsumPool(es, nc, "g_pp", 3, [128, 512], F32)
    poolA = PsumPool(es, nc, "g_pa", 2, [128, 384], F32)
    psO = [es.enter_context(nc.psum_tensor(f"g_po{i}", [128, 512], F32))
           for i in range(2)]
    psD = es.enter_context(nc.psum_tensor("g_pd", [128, 128], F32))

    sC = p.dma_sem("gc")
    sCg = p.dma_sem("gcg")
    sW = p.dma_sem("gw")
    sX = [p.dma_sem("gx0"), p.dma_sem("gx1")]
    sXR = p.dma_sem("gxr")
    sO = p.dma_sem("go")

    p.dma("q", ident[:, :], C["c_ident"], sC)
    p.dma("q", identf[:, :], C["c_identf"], sC)
    p.dma("q", mtril[:, :], C["c_mtril16"], sC)
    p.dma("q", m16off[:, :], C["c_m16off"], sC)
    p.dma("q", mpair[:, :], C["c_mpair"], sC)
    tCq = p.dma("q", mcols[:, :], C["c_mcols"], sC)
    p.emit("g", lambda g: g.memset(onesrow[:, :], 1.0), inc=True)
    p.emit("g", lambda g: g.memset(zrow1[:, :], 0.0), inc=True)
    p.emit("g", lambda g: g.memset(zr128[:, :], 0.0), inc=True)
    p.emit("g", lambda g: g.memset(epsln[:, :], 1e-5), inc=True)
    p.emit("g", lambda g: g.memset(Sblk[:, :, :], 0.0), inc=True)
    p.emit("g", lambda g: g.memset(base16[:, :, :], 0.0), inc=True)
    p.emit("g", lambda g: g.memset(base32[:, :, :], 0.0), inc=True)
    tO1 = p.emit("g", lambda g: g.memset(base64[:, :, :], 0.0), inc=True)

    es2 = contextlib.ExitStack()
    wnat = es2.enter_context(nc.sbuf_tensor("g_wnat", [128, 8, 1024], BF16))
    b8f = es2.enter_context(nc.sbuf_tensor("g_b8f", [8, 128], F32))
    free_wnat = None
    for wi, (wname, rbase, dstT) in enumerate([
            ("key_w", 0, kwT),
            ("vq_w", 0, vqTv),
            ("vq_w", 8, vqTq),
            ("out_w", 0, owT),
    ]):
        tWn = p.dma("g", wnat[:, :, :],
                    W[wname].rearrange("(kt p) c -> p kt c", p=128)[
                        :, rbase:rbase + 8, :], sW,
                    w=[free_wnat])
        tes = []
        for a in range(8):
            for kt in range(8):
                ti_, tps, twar = poolP.get()
                tp = mm_transpose(
                    p, tps[:, 0:128],
                    wnat[:, a, 128 * kt:128 * (kt + 1)],
                    ident[:, :], w=[tWn, tCq, twar])
                te = p.emit("s", lambda s, kt=kt, a=a, dstT=dstT: s.copy(
                    dstT[:, kt, 128 * a:128 * (a + 1)], tps[:, 0:128]),
                    w=[tp], inc=True)
                poolP.done(ti_, te)
                tes.append(te)
        free_wnat = tes[-1]
    sKB = p.dma_sem("gkb")
    tBd = p.dma("q", b8f[:, :], W["key_b"].rearrange("(a p) -> a p", a=8),
                sKB)
    ti_, tps, twar = poolP.get()
    tpb = p.emit("t", lambda t: t.matmul(
        tps[:, 0:8], b8f[:, :], identf[0:8, 0:8],
        start=True, stop=True), w=[tBd, twar, tCq], inc=True)
    tkb = p.emit("s", lambda s: s.copy(kbcol[:, :], tps[:, 0:8]),
                 w=[tpb], inc=True)
    poolP.done(ti_, tkb)
    tob = p.dma("g", obrow[:, :], W["out_b"][None, :], sW,
                w=[free_wnat])
    es2.close()
    qt128d[1] = sb("g_qt128_1", [128, 8, 128], BF16)
    kh16d[1] = sb("g_kh16_1", [128, 8, 128], BF16)
    kh32d[1] = sb("g_kh32_1", [128, 8, 128], BF16)
    kh64d[1] = sb("g_kh64_1", [128, 8, 128], BF16)
    khstd[1] = sb("g_khst_1", [128, 8, 128], BF16)
    khrd[1] = sb("g_khr_1", [128, 8, 128], BF16)
    qzd[1] = [sb(f"g_qz{i}_1", [128, 8, 128], BF16) for i in range(6)]
    tprep = p.tok("s")

    xin2 = xin.rearrange("(n p) d -> n p d", p=128)
    xout2 = xout.rearrange("(n p) d -> n p d", p=128)
    nw_x = [None, None]
    free_lnt = [None]
    free_sec = {}
    fronts = {}
    vres = {}
    tsb_tok = {}
    tvtm = {}
    tsO_last = [None]

    def emit_front(sec):
        txl = []
        for tl in range(NCH):
            gtile = sec * NCH + tl
            tXc = p.dma("q", xt[tl][:, :], xin2[gtile], sX[tl],
                        w=[nw_x[tl]])
            tln = emit_ln_tile(p, xt[tl][:, :], lnt[:, :], stat,
                               w_src=[tXc, tO1], w_lnt=[free_lnt[0]],
                               epscol=epsln[:, :])
            nw_x[tl] = tln
            for m in range(8):
                ti_, tps, twar = poolP.get()
                tp = mm_transpose(p, tps[:, 0:128],
                                  lnt[:, 128 * m:128 * (m + 1)],
                                  identf[:, :], w=[tln, twar])
                te = p.emit("s", lambda s, m=m, tl=tl: s.copy(
                    xlnT[:, m, 128 * tl:128 * (tl + 1)], tps[:, 0:128]),
                    w=[tp], inc=True)
                poolP.done(ti_, te)
                txl.append(te)
            free_lnt[0] = tp
        txln = txl[-1]
        tks = []
        for m in range(8):
            ti_, tps, twar = poolP.get()
            tm = None
            for kt in range(8):
                tm = p.emit("t", lambda t, k_=kt, m=m: t.matmul(
                    tps[:, 0:TS], kwT[:, k_, 128 * m:128 * (m + 1)],
                    xlnT[:, k_, :], start=(k_ == 0), stop=(k_ == 7)),
                    w=[txln, tprep, twar] if kt == 0 else (), inc=(kt == 7))
            tk = p.emit("s", lambda s, m=m: s.activation(
                ksec[:, m, :], tps[:, 0:TS], AF.Sigmoid,
                bias=kbcol[:, m:m + 1]),
                w=[tm, free_sec.get("ksec")], inc=True)
            poolP.done(ti_, tk)
            tks.append(tk)
        tlgs = []
        tom = None
        for m in range(8):
            tlgs.append(p.emit("s", lambda s, m=m: s.activation(
                logk[:, m, :], ksec[:, m, :], AF.Ln),
                w=[tks[m], free_sec.get("logk")], inc=True))
            tom = p.emit("v", lambda v, m=m: v.tensor_scalar(
                omk[:, m, :], ksec[:, m, :], -1.0, 1.0,
                op0=ALU.mult, op1=ALU.add),
                w=[tks[m], free_sec.get("omk")], inc=True)
        free_sec["ksec"] = tom
        tqdm = None
        for m in range(8):
            ti_, tps, twar = poolP.get()
            tm = None
            for kt in range(8):
                tm = p.emit("t", lambda t, k_=kt, m=m: t.matmul(
                    tps[:, 0:TS], vqTq[:, k_, 128 * m:128 * (m + 1)],
                    xlnT[:, k_, :], start=(k_ == 0), stop=(k_ == 7)),
                    w=[txln, tprep, twar] if kt == 0 else (), inc=(kt == 7))
            tqdm = p.emit("s", lambda s, m=m: s.copy(qdm[:, m, :],
                                                     tps[:, 0:TS]),
                          w=[tm, free_sec.get("qdm")], inc=True)
            poolP.done(ti_, tqdm)
        tL = None
        for m in range(8):
            for c in range(NCH):
                tL = p.emit("v", lambda v, m=m, c=c: v.tensor_tensor_scan(
                    Lc[:, m, 128 * c:128 * (c + 1)],
                    logk[:, m, 128 * c:128 * (c + 1)],
                    zr128[:, :], 0.0,
                    op0=ALU.add, op1=ALU.add),
                    w=[tlgs[m], tO1, free_sec.get("L"),
                       free_sec.get("Lv")], inc=True)
        free_sec["logk"] = tL
        return dict(txln=txln, tqdm=tqdm, tL=tL, tom=tom)

    def emit_vproj(sec):
        fr = fronts[sec]
        te = None
        for tl in range(NCH):
            for bank in range(2):
                ti_, tps, twar = poolP.get()
                tm = None
                for kt in range(8):
                    tm = p.emit("t", lambda t, k_=kt, tl=tl, bank=bank:
                                t.matmul(
                        tps[:, 0:512],
                        xlnT[:, k_, 128 * tl:128 * (tl + 1)],
                        vqTv[:, k_, 512 * bank:512 * (bank + 1)],
                        start=(k_ == 0), stop=(k_ == 7)),
                        w=[fr["txln"], tprep, twar] if kt == 0 else (),
                        inc=(kt == 7))
                te = p.emit("s", lambda s, tl=tl, bank=bank: s.copy(
                    vtm[:, tl, 512 * bank:512 * (bank + 1)], tps[:, 0:512]),
                    w=[tm, free_sec.get("vtm"), free_sec.get("vtm2")],
                    inc=True)
                poolP.done(ti_, te)
        tvtm[sec] = te

    def emit_var_q(sec, c, fr):
        cp = c % 2
        qt128 = qt128d[cp]
        tL = fr["tL"]
        Lch3 = Lc[:, :, 128 * c:128 * (c + 1)].rearrange(
            "p m (b s) -> p m b s", s=16)
        qch = qdm[:, :, 128 * c:128 * (c + 1)]
        tb1 = p.emit("v", lambda v: v.tensor_copy(
            base16[:, :, 1:8], Lc[:, :, 128 * c + 15:128 * c + 127:16]),
            w=[tL, tO1, free_sec.get("b16")], inc=True)
        tb2 = p.emit("v", lambda v: v.tensor_copy(
            base32[:, :, 1:4], Lc[:, :, 128 * c + 31:128 * c + 127:32]),
            w=[tL], inc=True)
        tb3 = p.emit("v", lambda v: v.tensor_copy(
            base64[:, :, 1:2], Lc[:, :, 128 * c + 63:128 * c + 64]),
            w=[tL], inc=True)
        ta = p.emit("v", lambda v: v.tensor_tensor(
            argb.ap().rearrange("p m (b s) -> p m b s", s=16), Lch3,
            base16[:, :, :, None].broadcast_to([128, 8, 8, 16]),
            op=ALU.subtract), w=[tb1, free_sec.get("arg")], inc=True)
        te_ = p.emit("s", lambda s: s.activation(
            ebuf[:, :, :], argb[:, :, :], AF.Exp),
            w=[ta, free_sec.get("ebuf")], inc=True)
        t16 = p.emit("v", lambda v: v.tensor_mul(
            qh16[:, :, :], qch, ebuf[:, :, :]),
            w=[te_, fr["tqdm"], free_sec.get("qh16")], inc=True)
        tf1 = p.emit("v", lambda v: v.tensor_tensor(
            fac.ap().rearrange("p m (a b) -> p m a b", b=2),
            base16.ap().rearrange("p m (a b) -> p m a b", b=2),
            base32[:, :, :, None].broadcast_to([128, 8, 4, 2]),
            op=ALU.subtract), w=[tb2, tb1, free_sec.get("fac")],
            inc=True)
        tf1e = p.emit("s", lambda s: s.activation(
            fac[:, :, :], fac[:, :, :], AF.Exp), w=[tf1], inc=True)
        t32 = p.emit("v", lambda v: v.tensor_tensor(
            qh32.ap().rearrange("p m (b s) -> p m b s", s=16),
            qh16.ap().rearrange("p m (b s) -> p m b s", s=16),
            fac[:, :, :, None].broadcast_to([128, 8, 8, 16]),
            op=ALU.mult), w=[t16, tf1e, free_sec.get("qh32")], inc=True)
        tf2 = p.emit("v", lambda v: v.tensor_tensor(
            fac.ap().rearrange("p m (a b) -> p m a b", b=2)[:, :, 0:2],
            base32.ap().rearrange("p m (a b) -> p m a b", b=2),
            base64[:, :, :, None].broadcast_to([128, 8, 2, 2]),
            op=ALU.subtract), w=[t32, tb3], inc=True)
        tf2e = p.emit("s", lambda s: s.activation(
            fac[:, :, 0:4], fac[:, :, 0:4], AF.Exp), w=[tf2], inc=True)
        t64 = p.emit("v", lambda v: v.tensor_tensor(
            qh64.ap().rearrange("p m (b s) -> p m b s", s=32),
            qh32.ap().rearrange("p m (b s) -> p m b s", s=32),
            fac[:, :, 0:4, None].broadcast_to([128, 8, 4, 32]),
            op=ALU.mult), w=[tf2e, free_sec.get("qh64")], inc=True)
        tf3e = p.emit("s", lambda s: s.activation(
            fac[:, :, 0:2], base64[:, :, :], AF.Exp),
            w=[t64, tb3], inc=True)
        t128 = p.emit("v", lambda v: v.tensor_tensor(
            qt128.ap().rearrange("p m (b s) -> p m b s", s=64),
            qh64.ap().rearrange("p m (b s) -> p m b s", s=64),
            fac[:, :, 0:2, None].broadcast_to([128, 8, 2, 64]),
            op=ALU.mult), w=[tf3e, free_sec.get(f"qt128{cp}")], inc=True)
        return dict(cp=cp, t128=t128, te_=te_, t16=t16, t32=t32, t64=t64,
                    tb1=tb1)

    def emit_var_k(sec, c, fr, vq):
        cp = c % 2
        kh16 = kh16d[cp]
        kh32 = kh32d[cp]
        kh64 = kh64d[cp]
        khst = khstd[cp]
        khr = khrd[cp]
        qz = qzd[cp]
        tL = fr["tL"]
        te_, t16, t32, t64, t128, tb1 = (vq["te_"], vq["t16"], vq["t32"],
                                         vq["t64"], vq["t128"], vq["tb1"])
        Lch3 = Lc[:, :, 128 * c:128 * (c + 1)].rearrange(
            "p m (b s) -> p m b s", s=16)
        omch = omk[:, :, 128 * c:128 * (c + 1)]
        nb16 = Lc[:, :, 128 * c + 15:128 * c + 128:16]
        nb32 = Lc[:, :, 128 * c + 31:128 * c + 128:32]
        nb64 = Lc[:, :, 128 * c + 63:128 * c + 128:64]
        tak = p.emit("v", lambda v: v.tensor_tensor(
            argb.ap().rearrange("p m (b s) -> p m b s", s=16),
            Lch3, nb16[:, :, :, None].broadcast_to([128, 8, 8, 16]),
            op=ALU.subtract), w=[te_, t16], inc=True)
        tek = p.emit("s", lambda s: s.activation(
            ebuf[:, :, :], argb[:, :, :], AF.Exp, scale=-1.0),
            w=[tak, t16], inc=True)
        tk16 = p.emit("v", lambda v: v.tensor_mul(
            kh16[:, :, :], omch, ebuf[:, :, :]),
            w=[tek, fr["tom"], free_sec.get(f"kh16{cp}")], inc=True)
        tg1 = p.emit("v", lambda v: v.tensor_tensor(
            fac2.ap().rearrange("p m (a b) -> p m a b", b=2),
            nb16.rearrange("p m (a b) -> p m a b", b=2),
            nb32[:, :, :, None].broadcast_to([128, 8, 4, 2]),
            op=ALU.subtract), w=[tk16, free_sec.get("fac2")], inc=True)
        tg1e = p.emit("s", lambda s: s.activation(
            fac2[:, :, :], fac2[:, :, :], AF.Exp, scale=-1.0),
            w=[tg1], inc=True)
        tk32 = p.emit("v", lambda v: v.tensor_tensor(
            kh32.ap().rearrange("p m (b s) -> p m b s", s=16),
            kh16.ap().rearrange("p m (b s) -> p m b s", s=16),
            fac2[:, :, :, None].broadcast_to([128, 8, 8, 16]),
            op=ALU.mult), w=[tg1e, free_sec.get(f"kh32{cp}")], inc=True)
        tg2 = p.emit("v", lambda v: v.tensor_tensor(
            fac2.ap().rearrange("p m (a b) -> p m a b", b=2)[:, :, 0:2],
            nb32.rearrange("p m (a b) -> p m a b", b=2),
            nb64[:, :, :, None].broadcast_to([128, 8, 2, 2]),
            op=ALU.subtract), w=[tk32], inc=True)
        tg2e = p.emit("s", lambda s: s.activation(
            fac2[:, :, 0:4], fac2[:, :, 0:4], AF.Exp, scale=-1.0),
            w=[tg2], inc=True)
        tk64 = p.emit("v", lambda v: v.tensor_tensor(
            kh64.ap().rearrange("p m (b s) -> p m b s", s=32),
            kh32.ap().rearrange("p m (b s) -> p m b s", s=32),
            fac2[:, :, 0:4, None].broadcast_to([128, 8, 4, 32]),
            op=ALU.mult), w=[tg2e, free_sec.get(f"kh64{cp}")], inc=True)
        tg3 = p.emit("v", lambda v: v.tensor_tensor(
            fac2[:, :, 0:2], nb64,
            Lc[:, :, 128 * c + 127:128 * c + 128].broadcast_to(
                [128, 8, 2]),
            op=ALU.subtract), w=[tk64], inc=True)
        tg3e = p.emit("s", lambda s: s.activation(
            fac2[:, :, 0:2], fac2[:, :, 0:2], AF.Exp, scale=-1.0),
            w=[tg3], inc=True)
        tkst = p.emit("v", lambda v: v.tensor_tensor(
            khst.ap().rearrange("p m (b s) -> p m b s", s=64),
            kh64.ap().rearrange("p m (b s) -> p m b s", s=64),
            fac2[:, :, 0:2, None].broadcast_to([128, 8, 2, 64]),
            op=ALU.mult), w=[tg3e, free_sec.get(f"khst{cp}")], inc=True)
        tr1 = p.emit("v", lambda v: v.tensor_sub(
            fac2[:, :, 0:8], base16[:, :, :], nb16),
            w=[tkst, tb1], inc=True)
        tr1c = p.emit("v", lambda v: v.tensor_scalar_min(
            fac2[:, :, 0:8], fac2[:, :, 0:8], 80.0), w=[tr1], inc=True)
        tr1e = p.emit("s", lambda s: s.activation(
            fac2[:, :, 0:8], fac2[:, :, 0:8], AF.Exp), w=[tr1c],
            inc=True)
        tkr = p.emit("v", lambda v: v.tensor_tensor(
            khr.ap().rearrange("p m (b s) -> p m b s", s=16),
            kh16.ap().rearrange("p m (b s) -> p m b s", s=16),
            fac2[:, :, 0:8, None].broadcast_to([128, 8, 8, 16]),
            op=ALU.mult), w=[tr1e, free_sec.get(f"khr{cp}")], inc=True)
        tzp = []
        for li, qsrc in enumerate((qh16, qh32, qh64)):
            for par in range(2):
                tzp.append(p.emit("v", lambda v, li=li, par=par,
                                  qsrc=qsrc: v.tensor_scalar(
                    qz[2 * li + par][:, :, :], qsrc[:, :, :],
                    mcols[:, par:par + 1], None, op0=ALU.mult),
                    w=[t16, t32, t64, free_sec.get(f"qz{cp}")],
                    inc=True))
        tzpl = tzp[-1]
        tpe = p.emit("s", lambda s: s.activation(
            pend[:, 8 * cp:8 * cp + 8], Lc[:, :, 128 * c + 127], AF.Exp),
            w=[tL, free_sec.get(f"pend{cp}")], inc=True)
        for kname, ktok in (("arg", tek), ("ebuf", tk16),
                            ("qh16", tzp[1]), ("qh32", tzp[3]),
                            ("qh64", tzp[5]), ("fac", t128),
                            ("fac2", tkr), ("b16", tkr)):
            free_sec[kname] = ktok
        if c == 1:
            free_sec["omk"] = tk16
            free_sec["qdm"] = t16
            free_sec["L"] = tpe
            free_sec["Lv"] = tkr
        return (cp, t128, tkr, tzpl, tk16, tk32, tk64, tkst, tpe)

    def emit_tsb():
        return p.emit("s", lambda s: s.activation(
            Sbf[:, :, :], Sblk[:, :, :], AF.Copy),
            w=[free_sec.get("Sbf"), tO1] + list(free_sec.get("Supd", [])),
            inc=True)

    def emit_rest(sec, c, vr, tsb):
        cp, t128, tkr, tzpl, tk16, tk32, tk64, tkst, tpe = vr
        qt128 = qt128d[cp]
        kh16 = kh16d[cp]
        kh32 = kh32d[cp]
        kh64 = kh64d[cp]
        khst = khstd[cp]
        khr = khrd[cp]
        qz = qzd[cp]
        tvt = tvtm[sec]
        hout_toks = []
        abuf_free = [None] * 6
        pend_ia = []

        def flush_intra(ent):
            sl, kt_, par_, etoks, tmi_ = ent
            tival = None
            for pi_, tok_ in zip(range(3), etoks):
                tival = p.emit("t", lambda t, pi_=pi_, kt=kt_,
                               par=par_, sl=sl: t.matmul(
                    psO[kt // 4][:,
                                 128 * (kt % 4) + 64 * par:
                                 128 * (kt % 4) + 64 * par + 64],
                    abuf[sl + pi_][:, :],
                    vtm[:, c, 128 * kt + 64 * par:
                        128 * kt + 64 * par + 64],
                    start=False, stop=False, skip_group_check=True),
                    w=[tok_, tmi_, tvt], inc=True)
                abuf_free[sl + pi_] = tival
            hout_toks.append(tival)

        tmi = None
        ii = 0
        for kt in range(8):
            while pend_ia:
                flush_intra(pend_ia.pop(0))
            tmi = p.emit("t", lambda t, kt=kt: t.matmul(
                psO[kt // 4][:, 128 * (kt % 4):128 * (kt % 4) + 128],
                qt128[:, kt, :], Sbf[:, kt, :],
                start=True, stop=False, skip_group_check=True),
                w=[t128, tsb, free_sec.get(f"psO{kt // 4}")], inc=True)
            for par in range(2):
                if 40 <= stage < 41:
                    continue
                ai, aps, awar = poolA.get()
                sl = (ii % 2) * 3
                td_ = p.emit("t", lambda t, kt=kt, par=par: t.matmul(
                    aps[:, 0:128], khr[:, kt, :], qz[par][:, kt, :],
                    start=True, stop=True),
                    w=[tkr, tzpl, awar], inc=True)
                tz_ = p.emit("t", lambda t: t.matmul(
                    aps[:, 128:256], zrow1[:, :], zrow1[:, :],
                    start=True, stop=False, skip_group_check=True),
                    w=[], inc=True)
                tl64 = p.emit("t", lambda t, kt=kt, par=par: t.matmul(
                    aps[0:64, 192:256], kh64[:, kt, 0:64],
                    qz[4 + par][:, kt, 64:128],
                    start=False, stop=False, skip_group_check=True),
                    w=[tk64], inc=True)
                tl32a = p.emit("t", lambda t, kt=kt, par=par: t.matmul(
                    aps[0:32, 160:192], kh32[:, kt, 0:32],
                    qz[2 + par][:, kt, 32:64],
                    start=False, stop=False, skip_group_check=True),
                    w=[tk32], inc=True)
                tl32b = p.emit("t", lambda t, kt=kt, par=par: t.matmul(
                    aps[64:96, 224:256], kh32[:, kt, 64:96],
                    qz[2 + par][:, kt, 96:128],
                    start=False, stop=True, skip_group_check=True),
                    w=[], inc=True)
                to16 = p.emit("t", lambda t, kt=kt, par=par: t.matmul(
                    aps[:, 256:384], kh16[:, kt, :], qz[par][:, kt, :],
                    start=True, stop=True),
                    w=[tk16], inc=True)
                e0 = p.emit("v", lambda v, sl=sl: v.tensor_tensor(
                    abuf[sl][:, :], aps[:, 0:128], mtril[:, :],
                    op=ALU.mult), w=[to16, td_, abuf_free[sl]], inc=True)
                e1 = p.emit("v", lambda v, sl=sl: v.tensor_copy(
                    abuf[sl + 1][:, :], aps[:, 128:256]),
                    w=[to16, tl32b, tl64, tz_, abuf_free[sl + 1]],
                    inc=True)
                e2 = p.emit("v", lambda v, sl=sl: v.tensor_tensor(
                    abuf[sl + 2][:, :], aps[:, 256:384], m16off[:, :],
                    op=ALU.mult), w=[to16, abuf_free[sl + 2]], inc=True)
                poolA.done(ai, e2)
                pend_ia.append((sl, kt, par, (e0, e1, e2), tmi))
                if len(pend_ia) >= 2:
                    flush_intra(pend_ia.pop(0))
                ii += 1
        while pend_ia:
            flush_intra(pend_ia.pop(0))

        the = []
        for half in range(2):
            te2 = p.emit("s", lambda s, half=half: s.copy(
                htm[:, c, 512 * half:512 * (half + 1)],
                psO[half][:, 0:512]),
                w=hout_toks + [free_sec.get("htm")], inc=True)
            the.append(te2)
        free_sec["psO0"] = the[0]
        free_sec["psO1"] = the[1]

        tkhtm = []
        for kt in range(8):
            ti_, tps, twar = poolP.get()
            tp = mm_transpose(p, tps[:, 0:128], khst[:, kt, :],
                              ident[:, :], w=[tkst, twar])
            te2 = p.emit("s", lambda s, kt=kt: s.copy(
                khtm[:, kt, :], tps[:, 0:128]),
                w=[tp, free_sec.get("khtm")], inc=True)
            poolP.done(ti_, te2)
            tkhtm.append(te2)
        supd = []
        for kt in range(8):
            tdS = p.emit("t", lambda t, kt=kt: t.matmul(
                psD[:, 0:128], khtm[:, kt, :],
                vtm[:, c, 128 * kt:128 * (kt + 1)],
                start=True, stop=True),
                w=[tkhtm[kt], free_sec.get("psD"), tvt], inc=True)
            ts1 = p.emit("v", lambda v, kt=kt: v.tensor_scalar(
                Sblk[:, kt, :], Sblk[:, kt, :],
                pend[:, 8 * cp + kt:8 * cp + kt + 1],
                None, op0=ALU.mult), w=[tpe, tsb, tmi], inc=True)
            ts2 = p.emit("v", lambda v: v.tensor_tensor(
                stmp[:, :], psD[:, 0:128], mpair[:, :], op=ALU.mult),
                w=[tdS, free_sec.get("stmp")], inc=True)
            free_sec["psD"] = ts2
            ts4 = p.emit("v", lambda v, kt=kt: v.tensor_add(
                Sblk[:, kt, :], Sblk[:, kt, :], stmp[:, :]),
                w=[ts2, ts1], inc=True)
            free_sec["stmp"] = ts4
            supd.append(ts4)
        free_sec["Supd"] = supd
        free_sec["vtm"] = supd[-1]
        free_sec["vtm2"] = hout_toks[-1]
        for kname, ktok in ((f"qt128{cp}", hout_toks[-1]),
                            (f"kh16{cp}", hout_toks[-1]),
                            (f"kh32{cp}", hout_toks[-1]),
                            (f"kh64{cp}", hout_toks[-1]),
                            (f"khst{cp}", tkhtm[-1]),
                            (f"khr{cp}", hout_toks[-1]),
                            (f"qz{cp}", hout_toks[-1]),
                            (f"pend{cp}", supd[-1])):
            free_sec[kname] = ktok

    def emit_tail(sec):
        tes = []
        for tl in range(NCH):
            for m in range(8):
                ti_, tps, twar = poolP.get()
                tp = mm_transpose(p, tps[:, 0:128],
                                  htm[:, tl, 128 * m:128 * (m + 1)],
                                  ident[:, :],
                                  w=[free_sec.get("psO1"), twar,
                                     free_sec.get("hm_t")])
                te2 = p.emit("s", lambda s, m=m, tl=tl: s.copy(
                    hm[:, m, 128 * tl:128 * (tl + 1)], tps[:, 0:128]),
                    w=[tp, free_sec.get("hm")], inc=True)
                poolP.done(ti_, te2)
                tes.append(te2)
        thm = tes[-1]
        free_sec["htm"] = tes[-1]
        for tl in range(NCH):
            gtile = sec * NCH + tl
            tXR_ = p.dma("q", xt[tl][:, :], xin2[gtile], sXR,
                         w=[nw_x[tl]])
            dtoks = []
            for bank in range(2):
                ti_, tps, twar = poolP.get()
                tm = None
                for kt in range(8):
                    tm = p.emit("t", lambda t, k_=kt, tl=tl, bank=bank:
                                t.matmul(
                        tps[:, 0:512], hm[:, k_, 128 * tl:128 * (tl + 1)],
                        owT[:, k_, 512 * bank:512 * (bank + 1)],
                        start=(k_ == 0), stop=False),
                        w=[thm, twar] if kt == 0 else (), inc=False)
                tm = p.emit("t", lambda t, bank=bank: t.matmul(
                    tps[:, 0:512], onesrow[:, 0:128],
                    obrow[:, 512 * bank:512 * (bank + 1)],
                    start=False, stop=True), w=[tob], inc=True)
                tv = p.emit("v", lambda v, bank=bank, tl=tl:
                            v.scalar_tensor_tensor(
                    x3t[:, 512 * bank:512 * (bank + 1)], tps[:, 0:512],
                    1.0, xt[tl][:, 512 * bank:512 * (bank + 1)],
                    op0=ALU.mult, op1=ALU.add),
                    w=[tm, tXR_, free_sec.get("x3t")], inc=True)
                poolP.done(ti_, tv)
                dtoks.append(tv)
            nw_x[tl] = dtoks[-1]
            tsO = p.dma("q", xout2[gtile], x3t[:, :], sO, w=dtoks)
            free_sec["x3t"] = tsO
            tsO_last[0] = tsO
        free_sec["hm"] = p.tok("t")
        free_sec["hm_t"] = None

    fronts[0] = emit_front(0)
    emit_vproj(0)
    vq0 = emit_var_q(0, 0, fronts[0])
    vres[(0, 0)] = emit_var_k(0, 0, fronts[0], vq0)
    tsb_tok[(0, 0)] = emit_tsb()
    vq1 = emit_var_q(0, 1, fronts[0])
    vres[(0, 1)] = emit_var_k(0, 1, fronts[0], vq1)
    for sec in range(NSEC):
        emit_rest(sec, 0, vres[(sec, 0)], tsb_tok[(sec, 0)])
        if sec + 1 < NSEC:
            fronts[sec + 1] = emit_front(sec + 1)
            vres[(sec + 1, 0)] = emit_var_q(sec + 1, 0, fronts[sec + 1])
        tsb_tok[(sec, 1)] = emit_tsb()
        emit_rest(sec, 1, vres[(sec, 1)], tsb_tok[(sec, 1)])
        if sec + 1 < NSEC:
            emit_vproj(sec + 1)
            vres[(sec + 1, 0)] = emit_var_k(sec + 1, 0, fronts[sec + 1],
                                            vres[(sec + 1, 0)])
        emit_tail(sec)
        if sec + 1 < NSEC:
            tsb_tok[(sec + 1, 0)] = emit_tsb()
            vq = emit_var_q(sec + 1, 1, fronts[sec + 1])
            vres[(sec + 1, 1)] = emit_var_k(sec + 1, 1, fronts[sec + 1],
                                            vq)

    p.wait("q", [tsO_last[0]])
    return tsO_last[0]


def build_l1_attn_program(T, stage=99):
    nc = bass.Bass("TRN2", target_bir_lowering=False)
    xin = nc.dram_tensor("xin", [T, D], F32, kind="ExternalInput").ap()
    W = {
        "key_w": nc.dram_tensor("key_w", [D, D], F32,
                                kind="ExternalInput").ap(),
        "key_b": nc.dram_tensor("key_b", [D], F32,
                                kind="ExternalInput").ap(),
        "vq_w": nc.dram_tensor("vq_w", [2048, D], F32,
                               kind="ExternalInput").ap(),
        "out_w": nc.dram_tensor("out_w", [D, D], F32,
                                kind="ExternalInput").ap(),
        "out_b": nc.dram_tensor("out_b", [D], F32,
                                kind="ExternalInput").ap(),
    }
    C = {k: nc.dram_tensor(k, v.shape, BF16 if v.dtype == BF else F32,
                           kind="ExternalInput").ap()
         for k, v in host_consts().items()}
    xout = nc.dram_tensor("xout", [T, D], F32, kind="ExternalOutput").ap()
    with contextlib.ExitStack() as es:
        p = Prog(nc, es)
        emit_l1_attn(p, es, T, xin, xout, W, C, stage)
    return nc


def build_l0_attn_program(T):
    nc = bass.Bass("TRN2", target_bir_lowering=False)
    xin = nc.dram_tensor("xin", [T, D], F32, kind="ExternalInput").ap()
    W = {n: nc.dram_tensor(n, [D, D], F32, kind="ExternalInput").ap()
         for n in ("Wq", "Wk", "Wv")}
    for n in ("bq", "bk", "bv"):
        W[n] = nc.dram_tensor(n, [D], F32, kind="ExternalInput").ap()
    C = {
        "c_ff": nc.dram_tensor("c_ff", [D, 1280], BF16,
                               kind="ExternalInput").ap(),
        "c_idft": nc.dram_tensor("c_idft", [1280, D], BF16,
                                 kind="ExternalInput").ap(),
        "c_ident": nc.dram_tensor("c_ident", [128, 128], BF16,
                                  kind="ExternalInput").ap(),
        "c_identf": nc.dram_tensor("c_identf", [128, 128], F32,
                                   kind="ExternalInput").ap(),
    }
    xout = nc.dram_tensor("xout", [T, D], F32, kind="ExternalOutput").ap()
    with contextlib.ExitStack() as es:
        p = Prog(nc, es)
        emit_l0_attn(p, es, T, xin, xout, W, C)
    return nc


def build_fused_program(T):
    nc = bass.Bass("TRN2", target_bir_lowering=False)
    x = nc.dram_tensor("x", [T, D], F32, kind="ExternalInput").ap()
    out = nc.dram_tensor("out", [T, D], F32, kind="ExternalOutput").ap()
    X1 = nc.dram_tensor("X1", [T, D], F32, kind="Internal").ap()
    X2 = nc.dram_tensor("X2", [T, D], F32, kind="Internal").ap()
    X3 = nc.dram_tensor("X3", [T, D], F32, kind="Internal").ap()

    WA = {n: nc.dram_tensor(n, [D, D], F32, kind="ExternalInput").ap()
          for n in ("Wq", "Wk", "Wv")}
    for n in ("bq", "bk", "bv"):
        WA[n] = nc.dram_tensor(n, [D], F32, kind="ExternalInput").ap()
    WF = {
        "router_w": nc.dram_tensor("router_w", [D, 64], F32,
                                   kind="ExternalInput").ap(),
        "router_b": nc.dram_tensor("router_b", [64], F32,
                                   kind="ExternalInput").ap(),
        "w1": nc.dram_tensor("w1", [64, D, 256], F32,
                             kind="ExternalInput").ap(),
        "b1": nc.dram_tensor("b1", [64, 256], F32,
                             kind="ExternalInput").ap(),
        "w2": nc.dram_tensor("w2", [64, 256, D], F32,
                             kind="ExternalInput").ap(),
        "b2": nc.dram_tensor("b2", [64, D], F32,
                             kind="ExternalInput").ap(),
        "sw1": nc.dram_tensor("sw1", [2, D, 2048], F32,
                              kind="ExternalInput").ap(),
        "sb1": nc.dram_tensor("sb1", [2, 2048], F32,
                              kind="ExternalInput").ap(),
        "sw2": nc.dram_tensor("sw2", [2, 2048, D], F32,
                              kind="ExternalInput").ap(),
        "sb2": nc.dram_tensor("sb2", [2, D], F32,
                              kind="ExternalInput").ap(),
    }
    WG = {
        "key_w": nc.dram_tensor("key_w", [D, D], F32,
                                kind="ExternalInput").ap(),
        "key_b": nc.dram_tensor("key_b", [D], F32,
                                kind="ExternalInput").ap(),
        "vq_w": nc.dram_tensor("vq_w", [2048, D], F32,
                               kind="ExternalInput").ap(),
        "out_w": nc.dram_tensor("out_w", [D, D], F32,
                                kind="ExternalInput").ap(),
        "out_b": nc.dram_tensor("out_b", [D], F32,
                                kind="ExternalInput").ap(),
    }
    WU = {
        "w_gate": nc.dram_tensor("w_gate", [2816, D], F32,
                                 kind="ExternalInput").ap(),
        "w_up": nc.dram_tensor("w_up", [2816, D], F32,
                               kind="ExternalInput").ap(),
        "w_down": nc.dram_tensor("w_down", [D, 2816], F32,
                                 kind="ExternalInput").ap(),
    }
    C = {k: nc.dram_tensor(k, v.shape, BF16 if v.dtype == BF else F32,
                           kind="ExternalInput").ap()
         for k, v in host_consts().items()}

    with contextlib.ExitStack() as es0:
        p = Prog(nc, es0)
        with contextlib.ExitStack() as es:
            emit_l0_attn(p, es, T, x, X1, WA, C)
            p.barrier()
        p.barrier()
        with contextlib.ExitStack() as es:
            emit_l0_ffn(p, es, T, X1, X2, WF, C)
            p.barrier()
        p.barrier()
        with contextlib.ExitStack() as es:
            emit_l1_attn(p, es, T, X2, X3, WG, C)
            p.barrier()
        p.barrier()
        with contextlib.ExitStack() as es:
            emit_l1_ffn(p, es, T, X3, out, WU, C)
            p.barrier()
    return nc


_PROG_CACHE = {}


def _get_programs(T):
    if T not in _PROG_CACHE:
        _PROG_CACHE[T] = (
            build_l0_attn_program(T),
            build_l0_ffn_program(T),
            build_l1_attn_program(T),
            build_l1_ffn_program(T),
        )
    return _PROG_CACHE[T]


FUSED = True


def kernel(**inputs):
    if FUSED:
        return kernel_fused(**inputs)
    return kernel_unfused(**inputs)


def kernel_fused(**inputs):
    x = np.ascontiguousarray(inputs["L0_attn_x"])
    B, T, d = x.shape
    assert d == D
    C = host_consts()
    key = ("fused", T)
    if key not in _PROG_CACHE:
        _PROG_CACHE[key] = build_fused_program(T)
    nc = _PROG_CACHE[key]
    cores = list(range(B))
    wmap = {
        "Wq": inputs["L0_attn_Wq"], "bq": inputs["L0_attn_bq"],
        "Wk": inputs["L0_attn_Wk"], "bk": inputs["L0_attn_bk"],
        "Wv": inputs["L0_attn_Wv"], "bv": inputs["L0_attn_bv"],
        "router_w": inputs["L0_ffn_router_w"],
        "router_b": inputs["L0_ffn_router_b"],
        "w1": inputs["L0_ffn_w1"], "b1": inputs["L0_ffn_b1"],
        "w2": inputs["L0_ffn_w2"], "b2": inputs["L0_ffn_b2"],
        "sw1": inputs["L0_ffn_sw1"], "sb1": inputs["L0_ffn_sb1"],
        "sw2": inputs["L0_ffn_sw2"], "sb2": inputs["L0_ffn_sb2"],
        "key_w": inputs["L1_attn_key_w"], "key_b": inputs["L1_attn_key_b"],
        "vq_w": inputs["L1_attn_vq_w"], "out_w": inputs["L1_attn_out_w"],
        "out_b": inputs["L1_attn_out_b"],
        "w_gate": inputs["L1_ffn_w_gate"], "w_up": inputs["L1_ffn_w_up"],
        "w_down": inputs["L1_ffn_w_down"],
    }
    wmap.update(C)
    ims = []
    for c in cores:
        im = {"x": x[c]}
        im.update(wmap)
        ims.append(im)
    res = run_bass_kernel_spmd(nc, ims, core_ids=cores)
    return np.stack([np.asarray(r["out"]) for r in res.results], axis=0)


def kernel_unfused(**inputs):
    x = np.ascontiguousarray(inputs["L0_attn_x"])
    B, T, d = x.shape
    assert d == D
    C = host_consts()
    ncs = _get_programs(T)
    cores = list(range(B))

    def run(nc, per_core_xin, wmap):
        ims = []
        for c in cores:
            im = {"xin": per_core_xin[c]}
            im.update(wmap)
            ims.append(im)
        res = run_bass_kernel_spmd(nc, ims, core_ids=cores)
        return [np.asarray(r["xout"]) for r in res.results]

    cc = {"c_ident": C["c_ident"], "c_identf": C["c_identf"]}
    x0 = [x[c] for c in cores]
    x1 = run(ncs[0], x0, {
        "Wq": inputs["L0_attn_Wq"], "bq": inputs["L0_attn_bq"],
        "Wk": inputs["L0_attn_Wk"], "bk": inputs["L0_attn_bk"],
        "Wv": inputs["L0_attn_Wv"], "bv": inputs["L0_attn_bv"],
        "c_ff": C["c_ff"], "c_idft": C["c_idft"], **cc})
    x2 = run(ncs[1], x1, {
        "router_w": inputs["L0_ffn_router_w"],
        "router_b": inputs["L0_ffn_router_b"],
        "w1": inputs["L0_ffn_w1"], "b1": inputs["L0_ffn_b1"],
        "w2": inputs["L0_ffn_w2"], "b2": inputs["L0_ffn_b2"],
        "sw1": inputs["L0_ffn_sw1"], "sb1": inputs["L0_ffn_sb1"],
        "sw2": inputs["L0_ffn_sw2"], "sb2": inputs["L0_ffn_sb2"], **cc})
    x3 = run(ncs[2], x2, {
        "key_w": inputs["L1_attn_key_w"], "key_b": inputs["L1_attn_key_b"],
        "vq_w": inputs["L1_attn_vq_w"], "out_w": inputs["L1_attn_out_w"],
        "out_b": inputs["L1_attn_out_b"],
        **{k: v for k, v in C.items()}})
    x4 = run(ncs[3], x3, {
        "w_gate": inputs["L1_ffn_w_gate"], "w_up": inputs["L1_ffn_w_up"],
        "w_down": inputs["L1_ffn_w_down"], **cc})
    return np.stack(x4, axis=0)
